# version 25
# baseline (speedup 1.0000x reference)
import numpy as np
from contextlib import ExitStack

import concourse.bass as bass
import concourse.mybir as mybir
from concourse.bass_utils import run_bass_kernel_spmd

F32 = mybir.dt.float32
BF16 = mybir.dt.bfloat16
AF = mybir.ActivationFunctionType
ALU = mybir.AluOpType

D = 2048
NH = 16
I = 8192
TPC = 2048
NT = 16
KC = 16
ICW = 1024
NIC = I // ICW
NQT = 4
QT = 4
EPS = 1e-5

ENGINES = ("tensor", "vector", "scalar", "gpsimd", "sync")


class Res:
    __slots__ = ("w", "r")

    def __init__(self):
        self.w = None
        self.r = []


def resarr(n):
    return [Res() for _ in range(n)]


class Prog:

    DMA_POOL = 10

    def __init__(self):
        self.items = {e: [] for e in ENGINES}
        self.cnt = {}
        self.marks = {e: {} for e in ENGINES}
        self.dma_rr = {e: 0 for e in ENGINES}

    def op(self, eng, fn, reads=(), writes=(), dma=False, extra_waits=()):
        toks = []
        for r in reads:
            if r.w is not None:
                toks.append(r.w)
        for w in writes:
            toks.extend(w.r)
            if w.w is not None:
                toks.append(w.w)
        toks.extend(extra_waits)
        need = {}
        for s, v in toks:
            if eng == "tensor" and s == "c_tensor":
                continue
            if v > need.get(s, -1):
                need[s] = v
        wm = self.marks[eng]
        waits = []
        for s, v in sorted(need.items()):
            if v > wm.get(s, 0):
                waits.append((s, v))
                wm[s] = v
        if dma:
            k = self.dma_rr[eng]
            self.dma_rr[eng] = (k + 1) % self.DMA_POOL
            sname = f"d_{eng}{k}"
            n = 16
            prev = self.cnt.get(sname, 0)
            if prev > wm.get(sname, 0):
                waits.append((sname, prev))
                wm[sname] = prev
        else:
            sname = "c_" + eng
            n = 1
        self.cnt[sname] = self.cnt.get(sname, 0) + n
        tok = (sname, self.cnt[sname])
        self.items[eng].append((waits, fn, sname, n))
        for r in reads:
            r.r.append(tok)
        for w in writes:
            w.w = tok
            w.r = []
        return tok

    def barrier_wait(self, eng, toks):
        need = {}
        for s, v in toks:
            if v > need.get(s, -1):
                need[s] = v
        wm = self.marks[eng]
        waits = []
        for s, v in sorted(need.items()):
            if v > wm.get(s, 0):
                waits.append((s, v))
                wm[s] = v
        self.items[eng].append((waits, None, None, 0))

    def realize(self, nc, es):
        sems = {name: es.enter_context(nc.semaphore(name)) for name in sorted(self.cnt)}
        blk = es.enter_context(nc.Block())
        for eng in ENGINES:
            ops = self.items[eng]
            if not ops:
                continue
            deco = getattr(blk, eng)

            def body(h, ops=ops):
                for waits, fn, sname, n in ops:
                    for s, v in waits:
                        h.wait_ge(sems[s], v)
                    if fn is not None:
                        ins = fn(h)
                        if sname is not None:
                            ins.then_inc(sems[sname], n)

            deco(body)


def handoff(new_res_list, old_res_list):
    toks = []
    for old in old_res_list:
        if old.w is not None:
            toks.append(old.w)
        toks.extend(old.r)
    for new in new_res_list:
        new.r = new.r + list(toks)


def bcast_last(ap, n):
    return bass.AP(ap.tensor, ap.offset, [list(d) for d in ap.ap[:-1]] + [[0, n]])


def build():
    nc = bass.Bass("TRN2", target_bir_lowering=False)

    x_d = nc.dram_tensor("x", [TPC, D], F32, kind="ExternalInput").ap()
    xprev_d = nc.dram_tensor("xprev", [TPC, D], F32, kind="ExternalInput").ap()
    wq_d = nc.dram_tensor("Wq", [D, D], F32, kind="ExternalInput").ap()
    wv_d = nc.dram_tensor("Wv", [D, D], F32, kind="ExternalInput").ap()
    wg_d = nc.dram_tensor("Wg", [D, D], F32, kind="ExternalInput").ap()
    wo_d = nc.dram_tensor("Wo", [D, D], F32, kind="ExternalInput").ap()
    fg_d = nc.dram_tensor("w_gate", [D, I], F32, kind="ExternalInput").ap()
    fu_d = nc.dram_tensor("w_up", [D, I], F32, kind="ExternalInput").ap()
    fd_d = nc.dram_tensor("w_down", [I, D], F32, kind="ExternalInput").ap()
    beta_r = nc.dram_tensor("beta_r", [1, NH], F32, kind="ExternalInput").ap()
    beta_c = nc.dram_tensor("beta_c", [NH, 1], F32, kind="ExternalInput").ap()
    lng_d = nc.dram_tensor("ln_g", [1, D], F32, kind="ExternalInput").ap()
    lnb_d = nc.dram_tensor("ln_b", [1, D], F32, kind="ExternalInput").ap()
    y_d = nc.dram_tensor("y", [TPC, D], F32, kind="ExternalOutput").ap()
    s_yaff = nc.dram_tensor("s_yaff", [NT, 128, D], BF16, kind="Internal").ap()
    s_ygT = nc.dram_tensor("s_ygT", [NT, 128, D], BF16, kind="Internal").ap()
    s_x2 = nc.dram_tensor("s_x2", [NT, 128, D], F32, kind="Internal").ap()
    s_xn3T = nc.dram_tensor("s_xn3T", [NT, 128, D], BF16, kind="Internal").ap()
    s_acc = nc.dram_tensor("s_acc", [NT, 128, D], F32, kind="Internal").ap()

    es = ExitStack()
    sb = lambda name, shape, dt: es.enter_context(nc.sbuf_tensor(name, shape, dt))
    psm = lambda name, shape, dt: es.enter_context(nc.psum_tensor(name, shape, dt))

    wslotA = sb("wslotA", [128, KC, D], BF16)
    wslotB = sb("wslotB", [128, KC, D], BF16)
    stg = [sb(f"stg{i}", [128, D], F32) for i in range(2)]
    fpool = [sb(f"fp{i}", [128, D], F32) for i in range(3)]
    bpool = [sb(f"bp{i}", [128, D], BF16) for i in range(4)]
    vst = sb("vst", [128, D], BF16)
    carry = sb("carry", [128, D], BF16)
    t_mat = sb("t_mat", [128, NH, 128], BF16)
    l3_mat = sb("l3_mat", [128, NH, 128], BF16)
    stats6 = [sb(f"stats6_{i}", [128, 24], F32) for i in range(2)]
    mv = [sb(f"mv{i}", [128, 2], F32) for i in range(2)]
    sd1 = [sb(f"sd1_{i}", [128, 1], F32) for i in range(2)]
    rstd = [sb(f"rstd{i}", [128, 1], F32) for i in range(2)]
    ln1m = sb("ln1m", [128, NT], F32)
    ln1r = sb("ln1r", [128, NT], F32)
    c_eps = sb("c_eps", [128, 1], F32)
    c_n87 = sb("c_n87", [128, 1], F32)
    c_nl128 = sb("c_nl128", [128, 1], F32)
    c_zero = sb("c_zero", [128, 1], F32)
    ones_row = sb("ones_row", [1, 128], F32)
    iota_e = sb("iota_e", [128, 128], F32)
    iota_m = sb("iota_m", [128, 128], F32)
    ident = sb("ident", [128, 128], F32)
    e127 = sb("e127", [128, 128], BF16)
    beta_row = sb("beta_row", [1, NH], F32)
    loglam_bc = sb("loglam_bc", [128, NH], F32)
    beta_col = sb("beta_col", [NH, 1], F32)
    loglam_col = sb("loglam_col", [NH, 1], F32)
    wlamT = sb("wlamT", [NH, 128], F32)
    xbarT = sb("xbarT", [128, KC, NH], BF16)
    psA = psm("psA", [128, D], F32)
    psB = psm("psB", [128, D], F32)

    lng_bc = stg[0][:].bitcast(BF16)[:, 0:D]
    lnb_bc = stg[0][:].bitcast(BF16)[:, D:2 * D]
    iota_t1 = iota_e
    fgc = wslotA[:, 0:8, :].rearrange("p a (b c) -> p (a b) c", c=ICW)
    fuc = wslotA[:, 8:16, :].rearrange("p a (b c) -> p (a b) c", c=ICW)
    fdc = wslotB[:, 0:8, :]
    stgF = [wslotB[:, 8 + 2 * k:10 + 2 * k, :].rearrange("p a b -> p (a b)")
            .bitcast(F32) for k in range(4)]
    p0buf = [t_mat[:].rearrange("p a b -> p (a b)"),
             l3_mat[:].rearrange("p a b -> p (a b)")]

    P = Prog()

    r_wA = resarr(KC)
    r_wB = resarr(KC)
    r_fgc = resarr(4)
    r_fuc = resarr(4)
    r_fdc = resarr(4)
    r_stgF = resarr(4)
    r_stg = resarr(2)
    r_fp = resarr(3)
    r_bp = resarr(6)
    r_p0b = resarr(2)
    r_stats = [Res(), Res()]
    r_ln1 = Res()
    r_const = Res()
    r_decay = Res()
    r_ident = Res()
    r_vst = Res()
    r_carry = Res()
    r_A = [Res(), Res()]
    r_B = [Res(), Res()]
    r_prol = Res()
    r_syaff = resarr(NT)
    r_sygT = resarr(NT)
    r_sx2 = resarr(NT)
    r_sxn3T = resarr(NT)
    r_sacc = resarr(NT)
    out_toks = []

    def setup():
        P.op("vector", lambda v: v.memset(c_eps[:], EPS), writes=[r_const])
        P.op("vector", lambda v: v.memset(c_n87[:], -87.0), writes=[r_const])
        P.op("vector", lambda v: v.memset(c_nl128[:], -float(np.log(128.0))), writes=[r_const])
        P.op("vector", lambda v: v.memset(c_zero[:], 0.0), writes=[r_const])
        P.op("vector", lambda v: v.memset(ones_row[:], 1.0), writes=[r_const])
        P.op("gpsimd", lambda g: g.iota(iota_e[:], pattern=[[1, 128]], base=0,
                                        channel_multiplier=-1,
                                        allow_small_or_imprecise_dtypes=True),
             writes=[r_ident])
        P.op("gpsimd", lambda g: g.affine_select(iota_m[:], iota_e[:], pattern=[[1, 128]],
                                                 base=0, channel_multiplier=-1,
                                                 compare_op=ALU.is_ge, fill=1e9),
             reads=[r_ident], writes=[r_ident])
        P.op("vector", lambda v: v.tensor_single_scalar(ident[:], iota_e[:], c_zero[:, 0:1],
                                                        ALU.is_equal),
             reads=[r_ident, r_const], writes=[r_ident])
        P.op("gpsimd", lambda g: g.iota(iota_t1[:], pattern=[[1, 128]], base=1,
                                        channel_multiplier=0,
                                        allow_small_or_imprecise_dtypes=True),
             reads=[r_ident], writes=[r_ident])
        P.op("vector", lambda v: v.memset(e127[:], 1.0), writes=[r_decay])
        P.op("gpsimd", lambda g: g.affine_select(e127[:], e127[:], pattern=[[0, 128]],
                                                 base=-127, channel_multiplier=1,
                                                 compare_op=ALU.is_ge, fill=0.0),
             reads=[r_decay], writes=[r_decay])
        P.op("sync", lambda s: s.dma_start(out=beta_row[:], in_=beta_r[:, :]), dma=True,
             writes=[r_decay])
        P.op("sync", lambda s: s.dma_start(out=beta_col[:], in_=beta_c[:, :]), dma=True,
             writes=[r_decay])
        P.op("scalar", lambda s: s.activation(beta_row[:], beta_row[:], AF.Sigmoid),
             reads=[r_decay], writes=[r_decay])
        P.op("scalar", lambda s: s.activation(beta_col[:], beta_col[:], AF.Sigmoid),
             reads=[r_decay], writes=[r_decay])
        P.op("scalar", lambda s: s.activation(beta_row[:], beta_row[:], AF.Ln),
             reads=[r_decay], writes=[r_decay])
        P.op("scalar", lambda s: s.activation(loglam_col[:], beta_col[:], AF.Ln),
             reads=[r_decay], writes=[r_decay])
        P.op("vector", lambda v: v.tensor_scalar_max(beta_row[:], beta_row[:], c_n87[0:1, 0:1]),
             reads=[r_const], writes=[r_decay])
        P.op("vector", lambda v: v.tensor_scalar_max(loglam_col[:], loglam_col[:], c_n87[0:NH, 0:1]),
             reads=[r_const], writes=[r_decay])
        P.op("tensor", lambda t: t.matmul(psA[:, 0:NH], ones_row[:], beta_row[:],
                                          start=True, stop=True),
             reads=[r_decay, r_const], writes=[r_A[0]])
        P.op("vector", lambda v: v.tensor_copy(loglam_bc[:], psA[:, 0:NH]),
             reads=[r_A[0]], writes=[r_decay, r_A[0]])
        for h in range(NH):
            P.op("scalar", lambda s, h=h: s.activation(t_mat[:, h, :], iota_m[:], AF.Exp,
                                                       scale=loglam_bc[:, h:h + 1]),
                 reads=[r_decay, r_ident], writes=[r_decay])
            P.op("scalar", lambda s, h=h: s.activation(l3_mat[:, h, :], iota_t1[:], AF.Exp,
                                                       scale=loglam_bc[:, h:h + 1],
                                                       bias=c_nl128[:, 0:1]),
                 reads=[r_decay, r_ident, r_const], writes=[r_decay])

    def load_square_weight(w_dram, slot_sb, slot_res, cast_eng=None):
        for kc in range(KC):
            for hh in range(2):
                load_half(slot_sb[:, kc, hh * 1024:(hh + 1) * 1024], slot_res[kc],
                          w_dram[kc * 128:(kc + 1) * 128, hh * 1024:(hh + 1) * 1024],
                          True)

    def ln_stats(src_ap, src_res, bank_tile=None, par=0):
        st6, mvp, sdp, rsp, rs = stats6[par], mv[par], sd1[par], rstd[par], r_stats[par]
        def f(v, src_ap=src_ap, st6=st6):
            ins = None
            for i in range(4):
                ins = v.bn_stats(st6[:, 6 * i:6 * i + 6], src_ap[:, 512 * i:512 * (i + 1)])
            return ins
        P.op("vector", f, reads=[src_res], writes=[rs])
        P.op("vector", lambda v, mvp=mvp, st6=st6: v.bn_aggr(mvp[:], st6[:]),
             reads=[rs], writes=[rs])
        P.op("scalar", lambda s, sdp=sdp, mvp=mvp: s.activation(sdp[:], mvp[:, 1:2], AF.Sqrt,
                                                                bias=c_eps[:, 0:1]),
             reads=[rs, r_const], writes=[rs])
        P.op("vector", lambda v, rsp=rsp, sdp=sdp: v.reciprocal(rsp[:], sdp[:]),
             reads=[rs], writes=[rs])
        if bank_tile is not None:
            P.op("vector", lambda v, t=bank_tile, mvp=mvp: v.tensor_copy(ln1m[:, t:t + 1],
                                                                         mvp[:, 0:1]),
                 reads=[rs], writes=[r_ln1])
            P.op("vector", lambda v, t=bank_tile, rsp=rsp: v.tensor_copy(ln1r[:, t:t + 1],
                                                                         rsp[:]),
                 reads=[rs], writes=[r_ln1])
        return mvp, rsp, rs

    def transpose_tile(src_ap, src_res, dst_sb, dst_res, evac_eng="scalar", width=D):
        nhalf = width // 1024
        for half in range(nhalf):
            def f(t, src_ap=src_ap, half=half):
                ins = None
                for j in range(8):
                    kc = half * 8 + j
                    ins = t.transpose(psA[:, half * 1024 + j * 128:half * 1024 + (j + 1) * 128],
                                      src_ap[:, kc * 128:(kc + 1) * 128], ident[:])
                return ins
            P.op("tensor", f, reads=[src_res, r_ident], writes=[r_A[half]])
            if evac_eng == "scalar":
                P.op("scalar", lambda s, dst_sb=dst_sb, half=half: s.copy(
                    dst_sb[:, half * 1024:(half + 1) * 1024],
                    psA[:, half * 1024:(half + 1) * 1024]),
                    reads=[r_A[half]], writes=[dst_res, r_A[half]])
            else:
                P.op("vector", lambda v, dst_sb=dst_sb, half=half: v.tensor_copy(
                    dst_sb[:, half * 1024:(half + 1) * 1024],
                    psA[:, half * 1024:(half + 1) * 1024]),
                    reads=[r_A[half]], writes=[dst_res, r_A[half]])

    def proj_mms(xT_ap, xT_res, w_sb, w_res, psum, psum_res, ns=(0, 4)):
        for kc in range(KC):
            def f(t, xT_ap=xT_ap, w_sb=w_sb, psum=psum, kc=kc, ns=ns):
                ins = None
                for n in range(ns[0], ns[1]):
                    ins = t.matmul(psum[:, n * 512:(n + 1) * 512], xT_ap[:, kc, :],
                                   w_sb[:, kc, n * 512:(n + 1) * 512],
                                   start=(kc == 0), stop=(kc == KC - 1))
                return ins
            P.op("tensor", f, reads=[xT_res, w_res[kc]], writes=psum_res)

    def lnab_bcast():
        P.op("sync", lambda s: s.dma_start(out=fpool[0][0:1, :], in_=lng_d[:, :]), dma=True,
             writes=[r_fp[0]])
        def bc_mm(tt, srcrow):
            ins = None
            for n in range(4):
                ins = tt.matmul(psA[:, n * 512:(n + 1) * 512], ones_row[:],
                                srcrow[:, n * 512:(n + 1) * 512], start=True, stop=True)
            return ins
        P.op("tensor", lambda t: bc_mm(t, fpool[0][0:1, :]),
             reads=[r_fp[0], r_const], writes=[r_A[0], r_A[1]])
        P.op("vector", lambda v: v.tensor_copy(lng_bc, psA[:]),
             reads=[r_A[0], r_A[1]], writes=[r_stg[0], r_A[0], r_A[1]])
        P.op("sync", lambda s: s.dma_start(out=fpool[0][0:1, :], in_=lnb_d[:, :]), dma=True,
             reads=[r_fp[0]], writes=[r_fp[0]])
        P.op("tensor", lambda t: bc_mm(t, fpool[0][0:1, :]),
             reads=[r_fp[0], r_const], writes=[r_A[0], r_A[1]])
        P.op("vector", lambda v: v.tensor_copy(lnb_bc, psA[:]),
             reads=[r_A[0], r_A[1]], writes=[r_stg[0], r_A[0], r_A[1], r_fp[0]])

    def carry_prologue():
        for t in range(NT):
            fx = t % 2
            P.op("sync", lambda s, t=t, fx=fx: s.dma_start(
                out=fpool[fx][:], in_=xprev_d[t * 128:(t + 1) * 128, :]),
                dma=True, writes=[r_fp[fx]])
            mvp, rsp, rs = ln_stats(fpool[fx][:], r_fp[fx], par=fx)
            P.op("vector", lambda v, fx=fx, mvp=mvp, rsp=rsp: v.tensor_scalar(
                fpool[2][:], fpool[fx][:], mvp[:, 0:1], rsp[:], ALU.subtract, ALU.mult),
                reads=[r_fp[fx], rs], writes=[r_fp[2], r_fp[fx]])
            P.op("vector", lambda v: v.tensor_copy(bpool[0][:], fpool[2][:]),
                 reads=[r_fp[2]], writes=[r_bp[0], r_fp[2]])
            P.op("gpsimd", lambda g, t=t: g.iota(wlamT[:], pattern=[[-1, 128]],
                                                 base=2047 - 128 * t, channel_multiplier=0,
                                                 allow_small_or_imprecise_dtypes=True),
                 writes=[r_prol])
            P.op("scalar", lambda s: s.activation(wlamT[:], wlamT[:], AF.Exp,
                                                  scale=loglam_col[:, 0:1]),
                 reads=[r_prol, r_decay], writes=[r_prol])
            P.op("tensor", lambda tt: tt.transpose(psB[:, 0:16], wlamT[:], ident[0:16, 0:16]),
                 reads=[r_prol, r_ident], writes=[r_B[0]])
            P.op("scalar", lambda s: s.copy(bpool[1][:, 0:16], psB[:, 0:16]),
                 reads=[r_B[0]], writes=[r_bp[1], r_B[0]])
            def f(tt, t=t):
                ins = None
                for n in range(4):
                    ins = tt.matmul(psA[0:NH, n * 512:(n + 1) * 512], bpool[1][:, 0:NH],
                                    bpool[0][:, n * 512:(n + 1) * 512],
                                    start=(t == 0), stop=(t == NT - 1))
                return ins
            P.op("tensor", f, reads=[r_bp[0], r_bp[1]], writes=[r_A[0], r_A[1]])
        P.op("scalar", lambda s: s.copy(fpool[2][0:NH, :], psA[0:NH, :]),
             reads=[r_A[0], r_A[1]], writes=[r_fp[2], r_A[0], r_A[1]])

        def trx(tt):
            ins = None
            for kc in range(KC):
                ins = tt.transpose(psB[:, kc * 16:(kc + 1) * 16],
                                   fpool[2][0:NH, kc * 128:(kc + 1) * 128], ident[0:16, 0:16])
            return ins
        P.op("tensor", trx, reads=[r_fp[2], r_ident], writes=[r_B[0]])
        P.op("scalar", lambda s: s.copy(xbarT[:].rearrange("p a b -> p (a b)"),
                                        psB[:, 0:KC * NH]),
             reads=[r_B[0]], writes=[r_prol, r_B[0], r_fp[2]])
        for h in range(NH):
            for kc in range(KC):
                sub = bpool[2][:, (kc % KC) * 128:(kc % KC + 1) * 128]
                src = xbarT[:, kc, h:h + 1]
                bc_ap = bcast_last(src, 128)
                P.op("vector", lambda v, sub=sub, bc_ap=bc_ap: v.tensor_copy(
                    sub.rearrange("p (a b) -> p a b", b=128), bc_ap),
                    reads=[r_prol], writes=[r_bp[2]])
                def f(tt, h=h, kc=kc, sub=sub):
                    return tt.matmul(psB[:, h * 128:(h + 1) * 128], sub,
                                     wslotB[:, kc, h * 128:(h + 1) * 128],
                                     start=(kc == 0), stop=(kc == KC - 1))
                P.op("tensor", f, reads=[r_bp[2], r_wB[kc]],
                     writes=[r_B[h // 8]])
        P.op("scalar", lambda s: s.copy(carry[:], psB[:]),
             reads=[r_B[0], r_B[1]], writes=[r_carry, r_B[0], r_B[1]])

    def attn_qv_scan():
        def prep(t):
            fx = t % 2
            P.op("sync", lambda s, t=t, fx=fx: s.dma_start(
                out=fpool[fx][:], in_=x_d[t * 128:(t + 1) * 128, :]), dma=True,
                writes=[r_fp[fx]])
            mvp, rsp, rs = ln_stats(fpool[fx][:], r_fp[fx], bank_tile=t, par=fx)
            P.op("vector", lambda v, fx=fx, mvp=mvp, rsp=rsp: v.tensor_scalar(
                fpool[2][:], fpool[fx][:], mvp[:, 0:1], rsp[:], ALU.subtract, ALU.mult),
                reads=[r_fp[fx], rs], writes=[r_fp[2]])

        prep(0)
        for t in range(NT):
            fx = t % 2
            bx = t % 2
            transpose_tile(fpool[2][:], r_fp[2], bpool[bx], r_bp[bx])
            xT = bpool[bx][:].rearrange("p (a b) -> p a b", b=128)
            proj_mms(xT, r_bp[bx], wslotA, r_wA, psB, [r_B[0], r_B[1]])
            proj_mms(xT, r_bp[bx], wslotB, r_wB, psA, [r_A[0], r_A[1]])
            if t + 1 < NT:
                prep(t + 1)
            P.op("scalar", lambda s: s.copy(vst[:], psA[:]),
                 reads=[r_A[0], r_A[1]], writes=[r_vst, r_A[0], r_A[1]])

            def scan_f(tt):
                ins = None
                for h in range(NH):
                    ins = tt.matmul(psA[:, h * 128:(h + 1) * 128], t_mat[:, h, :],
                                    vst[:, h * 128:(h + 1) * 128], start=True, stop=False)
                    ins = tt.matmul(psA[:, h * 128:(h + 1) * 128], l3_mat[:, h, :],
                                    carry[:, h * 128:(h + 1) * 128], start=False, stop=True)
                return ins
            P.op("tensor", scan_f, reads=[r_vst, r_carry, r_decay], writes=[r_A[0], r_A[1]])
            P.op("scalar", lambda s: s.copy(vst[:], psA[:]),
                 reads=[r_A[0], r_A[1]], writes=[r_vst])
            P.op("vector", lambda v: v.tensor_tensor(stg[1][:], psB[:], vst[:],
                                                     ALU.mult),
                 reads=[r_B[0], r_B[1], r_vst], writes=[r_stg[1], r_B[0], r_B[1]])
            if t + 1 < NT:
                def ext_f(tt):
                    ins = None
                    for n in range(4):
                        ins = tt.matmul(psA[:, n * 512:(n + 1) * 512], e127[:],
                                        vst[:, n * 512:(n + 1) * 512], start=True, stop=True)
                    return ins
                P.op("tensor", ext_f, reads=[r_vst, r_decay], writes=[r_A[0], r_A[1]])
                P.op("scalar", lambda s: s.copy(carry[:], psA[:]),
                     reads=[r_A[0], r_A[1]], writes=[r_carry, r_A[0], r_A[1]])
            mvp, rsp, rs = ln_stats(stg[1][:], r_stg[1], par=fx)
            P.op("vector", lambda v, mvp=mvp, rsp=rsp: v.tensor_scalar(
                stg[1][:], stg[1][:], mvp[:, 0:1], rsp[:],
                ALU.subtract, ALU.mult),
                 reads=[rs], writes=[r_stg[1]])
            P.op("vector", lambda v: v.tensor_tensor(stg[1][:], stg[1][:],
                                                     lng_bc, ALU.mult),
                 reads=[r_stg[0]], writes=[r_stg[1]])
            P.op("vector", lambda v: v.tensor_tensor(bpool[2][:], stg[1][:],
                                                     lnb_bc, ALU.add),
                 reads=[r_stg[0], r_stg[1]], writes=[r_bp[2], r_stg[1]])
            P.op("gpsimd", lambda g, t=t: g.dma_start(out=s_yaff[t], in_=bpool[2][:]),
                 dma=True, reads=[r_bp[2]], writes=[r_syaff[t]])

    def attn_gate():
        load_square_weight(wg_d, wslotA, r_wA)

        def drain_ygT(t):
            transpose_tile(fpool[t % 2][:], r_fp[t % 2], bpool[3], r_bp[3],
                           evac_eng="vector")
            P.op("gpsimd", lambda g, t=t: g.dma_start(out=s_ygT[t], in_=bpool[3][:]),
                 dma=True, reads=[r_bp[3]], writes=[r_sygT[t]])

        def prep(t):
            fx = t % 2
            P.op("sync", lambda s, t=t, fx=fx: s.dma_start(
                out=fpool[fx][:], in_=x_d[t * 128:(t + 1) * 128, :]), dma=True,
                writes=[r_fp[fx]])
            P.op("vector", lambda v, fx=fx, t=t: v.tensor_scalar(
                fpool[2][:], fpool[fx][:], ln1m[:, t:t + 1], ln1r[:, t:t + 1],
                ALU.subtract, ALU.mult),
                reads=[r_fp[fx], r_ln1], writes=[r_fp[2]])

        prep(0)
        pend = None
        for t in range(NT):
            fx = t % 2
            bx = t % 2
            transpose_tile(fpool[2][:], r_fp[2], bpool[bx], r_bp[bx])
            xT = bpool[bx][:].rearrange("p (a b) -> p a b", b=128)
            proj_mms(xT, r_bp[bx], wslotA, r_wA, psB, [r_B[0]], ns=(0, 2))
            P.op("scalar", lambda s, fx=fx: s.activation(fpool[fx][:, 0:1024],
                                                         psB[:, 0:1024], AF.Silu),
                 reads=[r_B[0]], writes=[r_fp[fx], r_B[0]])
            proj_mms(xT, r_bp[bx], wslotA, r_wA, psB, [r_B[1]], ns=(2, 4))
            P.op("scalar", lambda s, fx=fx: s.activation(fpool[fx][:, 1024:2048],
                                                         psB[:, 1024:2048], AF.Silu),
                 reads=[r_B[1]], writes=[r_fp[fx], r_B[1]])
            if pend is not None:
                drain_ygT(pend)
            if t + 1 < NT:
                prep(t + 1)
            P.op("gpsimd", lambda g, t=t: g.dma_start(out=bpool[2][:], in_=s_yaff[t]),
                 dma=True, reads=[r_syaff[t]], writes=[r_bp[2]])
            P.op("vector", lambda v, fx=fx: v.tensor_tensor(fpool[fx][:], bpool[2][:],
                                                            fpool[fx][:], ALU.mult),
                 reads=[r_bp[2]], writes=[r_fp[fx], r_bp[2]])
            pend = t
        drain_ygT(pend)

    def attn_out():
        load_square_weight(wo_d, wslotB, r_wB)

        def drain_xn3T(t):
            bx = t % 2
            pso, rpso = (psA, r_A) if t % 2 == 0 else (psB, r_B)
            for half in range(2):
                def ftr(tt, half=half, pso=pso):
                    ins = None
                    for j in range(8):
                        kc = half * 8 + j
                        ins = tt.transpose(
                            pso[:, half * 1024 + j * 128:half * 1024 + (j + 1) * 128],
                            fpool[2][:, kc * 128:(kc + 1) * 128], ident[:])
                    return ins
                P.op("tensor", ftr, reads=[r_fp[2], r_ident], writes=[rpso[half]])
                P.op("scalar", lambda s, half=half, pso=pso, bx=bx: s.copy(
                    bpool[bx][:, half * 1024:(half + 1) * 1024],
                    pso[:, half * 1024:(half + 1) * 1024]),
                    reads=[rpso[half]], writes=[r_bp[bx], rpso[half]])
            P.op("gpsimd", lambda g, t=t, bx=bx: g.dma_start(out=s_xn3T[t], in_=bpool[bx][:]),
                 dma=True, reads=[r_bp[bx]], writes=[r_sxn3T[t]])

        pend = None
        for t in range(NT):
            bx = t % 2
            fx = t % 2
            P.op("sync", lambda s, t=t, bx=bx: s.dma_start(out=bpool[bx][:], in_=s_ygT[t]),
                 dma=True, reads=[r_sygT[t]], writes=[r_bp[bx]])
            ygT = bpool[bx][:].rearrange("p (a b) -> p a b", b=128)
            pso, rpso = (psA, r_A) if t % 2 == 0 else (psB, r_B)
            proj_mms(ygT, r_bp[bx], wslotB, r_wB, pso, [rpso[0], rpso[1]])
            if pend is not None:
                drain_xn3T(pend)
            mvp, rsp, rs = ln_stats(pso[:], rpso[0], par=t % 2)
            P.op("vector", lambda v, pso=pso, mvp=mvp, rsp=rsp: v.tensor_scalar(
                fpool[2][:], pso[:], mvp[:, 0:1], rsp[:],
                ALU.subtract, ALU.mult),
                 reads=[rs, rpso[0], rpso[1]],
                 writes=[r_fp[2], rpso[0], rpso[1]])
            P.op("sync", lambda s, t=t, fx=fx: s.dma_start(
                out=fpool[fx][:], in_=x_d[t * 128:(t + 1) * 128, :]), dma=True,
                writes=[r_fp[fx]])
            P.op("vector", lambda v, fx=fx: v.tensor_tensor(fpool[fx][:], fpool[2][:],
                                                            fpool[fx][:], ALU.add),
                 reads=[r_fp[2]], writes=[r_fp[fx], r_fp[2]])
            P.op("gpsimd", lambda g, t=t, fx=fx: g.dma_start(out=s_x2[t], in_=fpool[fx][:]),
                 dma=True, reads=[r_fp[fx]], writes=[r_sx2[t]])
            mvp, rsp, rs = ln_stats(fpool[fx][:], r_fp[fx], par=t % 2)
            P.op("vector", lambda v, fx=fx, mvp=mvp, rsp=rsp: v.tensor_scalar(
                fpool[2][:], fpool[fx][:], mvp[:, 0:1], rsp[:], ALU.subtract, ALU.mult),
                reads=[r_fp[fx], rs], writes=[r_fp[2], r_fp[fx]])
            pend = t
        drain_xn3T(pend)

    wstg4 = [stg[0][:][:, 0:1024], stg[0][:][:, 1024:2048],
             stg[1][:][:, 0:1024], stg[1][:][:, 1024:2048]]
    r_wstg4 = resarr(4)
    wrr = [0]

    def load_half(dst_ap, dst_res, src_ap, couple):
        k = wrr[0] % 4
        wrr[0] += 1
        extra = [r_stg[k // 2]] if couple else []
        P.op("sync", lambda s, src_ap=src_ap, k=k: s.dma_start(
            out=wstg4[k], in_=src_ap), dma=True, writes=[r_wstg4[k]] + extra)
        if wrr[0] % 2 == 0:
            P.op("vector", lambda v, dst_ap=dst_ap, k=k: v.tensor_copy(
                dst_ap, wstg4[k]),
                reads=[r_wstg4[k]], writes=[dst_res, r_wstg4[k]])
        else:
            P.op("scalar", lambda s, dst_ap=dst_ap, k=k: s.copy(
                dst_ap, wstg4[k]),
                reads=[r_wstg4[k]], writes=[dst_res, r_wstg4[k]])

    def load_cast(dst_ap, dst_res, src_ap, width):
        for hh in range(width // 1024):
            load_half(dst_ap[:, hh * 1024:(hh + 1) * 1024], dst_res,
                      src_ap[:, hh * 1024:(hh + 1) * 1024], False)

    def load_q(ic, q):
        for kc in range(q * 4, (q + 1) * 4):
            load_cast(fgc[:, kc, :], r_fgc[q],
                      fg_d[kc * 128:(kc + 1) * 128, ic * ICW:(ic + 1) * ICW], ICW)
            load_cast(fuc[:, kc, :], r_fuc[q],
                      fu_d[kc * 128:(kc + 1) * 128, ic * ICW:(ic + 1) * ICW], ICW)

    def load_fdc(ic):
        for c in range(ICW // 128):
            load_cast(fdc[:, c, :], r_fdc[c // 2],
                      fd_d[(ic * ICW + c * 128):(ic * ICW + (c + 1) * 128), :], D)

    def ffn():
        hbuf = [p0buf[0].bitcast(F32), p0buf[1].bitcast(F32),
                fpool[2][:][:, 0:ICW]]
        hflats = [p0buf[0], p0buf[1], fpool[2][:].bitcast(BF16)[:, 0:2 * ICW]]
        r_h = [r_p0b[0], r_p0b[1], r_fp[2]]
        drr = [0]

        def emit_ln4(t):
            P.op("sync", lambda s, t=t: s.dma_start(out=fpool[2][:], in_=s_acc[t]),
                 dma=True, reads=[r_sacc[t]], writes=[r_fp[2]])
            mvp, rsp, rs = ln_stats(fpool[2][:], r_fp[2], par=t % 2)
            fx = t % 2
            P.op("sync", lambda s, t=t, fx=fx: s.dma_start(out=fpool[fx][:], in_=s_x2[t]),
                 dma=True, reads=[r_sx2[t]], writes=[r_fp[fx]])
            P.op("vector", lambda v, fx=fx, mvp=mvp, rsp=rsp: v.tensor_scalar(
                fpool[2][:], fpool[2][:], mvp[:, 0:1], rsp[:],
                ALU.subtract, ALU.mult),
                 reads=[rs], writes=[r_fp[2]])
            P.op("vector", lambda v, fx=fx: v.tensor_tensor(fpool[2][:], fpool[2][:],
                                                            fpool[fx][:], ALU.add),
                 reads=[r_fp[fx]], writes=[r_fp[2], r_fp[fx]])
            yres = Res()
            tok = P.op("sync", lambda s, t=t: s.dma_start(
                out=y_d[t * 128:(t + 1) * 128, :], in_=fpool[2][:]),
                dma=True, reads=[r_fp[2]], writes=[yres])
            out_toks.append(tok)

        def drain(pend):
            pt, pic = pend
            par = ((pic * NT + pt) % 3) if pic < NIC - 1 else (pt % 2)
            hsp = hbuf[par]
            hflat = hflats[par]
            rh = r_h[par]

            def trf(tt, hsp=hsp):
                ins = None
                for j in range(8):
                    ins = tt.transpose(psA[:, j * 128:(j + 1) * 128],
                                       hsp[:, j * 128:(j + 1) * 128], ident[:])
                return ins
            P.op("tensor", trf, reads=[rh, r_ident], writes=[r_A[0]])
            hT_flat = hflat
            rc = [Res(), Res()]
            for hh in range(2):
                P.op("scalar", lambda s, hT_flat=hT_flat, hh=hh: s.copy(
                    hT_flat[:, hh * 512:(hh + 1) * 512],
                    psA[:, hh * 512:(hh + 1) * 512]),
                     reads=[r_A[0]], writes=[rh, rc[hh], r_A[0]])
            hT = hT_flat[:, 0:ICW].rearrange("p (a b) -> p a b", b=128)
            for q in range(4):
                def f(tt, q=q, hT=hT):
                    ins = None
                    for kc in range(q * 2, (q + 1) * 2):
                        for n in range(4):
                            ins = tt.matmul(psB[:, n * 512:(n + 1) * 512], hT[:, kc, :],
                                            fdc[:, kc, n * 512:(n + 1) * 512],
                                            start=(kc == 0), stop=(kc == ICW // 128 - 1))
                    return ins
                P.op("tensor", f, reads=[rh, rc[q // 2], r_fdc[q]],
                     writes=[r_B[0], r_B[1]])
            dk = drr[0] % 4
            drr[0] += 1
            ev = stgF[dk]
            P.op("vector", lambda v, ev=ev: v.tensor_copy(ev[:], psB[:]),
                 reads=[r_B[0], r_B[1]], writes=[r_stgF[dk], r_B[0], r_B[1]])
            if pic == 0:
                P.op("gpsimd", lambda g, pt=pt, ev=ev: g.dma_start(
                    out=s_acc[pt], in_=ev[:]),
                    dma=True, reads=[r_stgF[dk]], writes=[r_sacc[pt]])
            else:
                P.op("gpsimd", lambda g, pt=pt, ev=ev: g.dma_start(
                    out=s_acc[pt], in_=ev[:], accum_op=ALU.add),
                    dma=True, reads=[r_stgF[dk]], writes=[r_sacc[pt]])
            if pic == NIC - 1:
                emit_ln4(pt)

        pending = []
        for ic in range(NIC):
            while pending:
                drain(pending.pop(0))
            load_fdc(ic)
            for t in range(NT):
                bx = t % 2
                fx = t % 2
                P.op("sync", lambda s, t=t, bx=bx: s.dma_start(out=bpool[bx][:],
                                                               in_=s_xn3T[t]),
                     dma=True, reads=[r_sxn3T[t]], writes=[r_bp[bx]])
                xT = bpool[bx][:].rearrange("p (a b) -> p a b", b=128)
                for q in range(4):
                    def f(tt, xT=xT, q=q):
                        ins = None
                        for kc in range(q * 4, (q + 1) * 4):
                            for n in range(2):
                                ins = tt.matmul(psA[:, n * 512:(n + 1) * 512], xT[:, kc, :],
                                                fgc[:, kc, n * 512:(n + 1) * 512],
                                                start=(kc == 0), stop=(kc == KC - 1))
                        return ins
                    P.op("tensor", f, reads=[r_bp[bx], r_fgc[q]], writes=[r_A[0]])
                P.op("scalar", lambda s, fx=fx: s.activation(fpool[fx][:, 0:ICW],
                                                             psA[:, 0:ICW], AF.Silu),
                     reads=[r_A[0]], writes=[r_fp[fx], r_A[0]])
                for q in range(4):
                    def f(tt, xT=xT, q=q):
                        ins = None
                        for kc in range(q * 4, (q + 1) * 4):
                            for n in range(2):
                                ins = tt.matmul(psA[:, 1024 + n * 512:1024 + (n + 1) * 512],
                                                xT[:, kc, :], fuc[:, kc, n * 512:(n + 1) * 512],
                                                start=(kc == 0), stop=(kc == KC - 1))
                        return ins
                    P.op("tensor", f, reads=[r_bp[bx], r_fuc[q]], writes=[r_A[1]])
                    if t == NT - 1 and ic + 1 < NIC:
                        load_q(ic + 1, q)
                hpar = ((ic * NT + t) % 3) if ic < NIC - 1 else (t % 2)
                hsl = hbuf[hpar]
                rht = r_h[hpar]
                P.op("vector", lambda v, fx=fx, hsl=hsl: v.tensor_tensor(
                    hsl[:, 0:ICW], psA[:, ICW:2 * ICW],
                    fpool[fx][:, 0:ICW], ALU.mult),
                     reads=[r_A[1], r_fp[fx]], writes=[rht, r_A[1], r_fp[fx]])
                maxp = 2 if ic < NIC - 1 else 1
                if len(pending) >= maxp:
                    drain(pending.pop(0))
                habs = bpool[2]
                P.op("scalar", lambda s, hsl=hsl: s.activation(habs[:, 0:ICW],
                                                               hsl[:, 0:ICW], AF.Abs),
                     reads=[rht], writes=[r_bp[2]])
                a4 = habs[:, 0:ICW].rearrange("p (g k) -> p g k", k=4)
                G = ICW // 4
                m2t = bpool[3]
                mn1 = m2t[:, 0:G].rearrange("p (g o) -> p g o", o=1)
                mx1 = m2t[:, G:2 * G].rearrange("p (g o) -> p g o", o=1)
                mn2 = m2t[:, 2 * G:3 * G].rearrange("p (g o) -> p g o", o=1)
                mx2 = m2t[:, 3 * G:4 * G].rearrange("p (g o) -> p g o", o=1)
                P.op("vector", lambda v: v.tensor_tensor(mn1, a4[:, :, 0:1], a4[:, :, 1:2],
                                                         ALU.min),
                     reads=[r_bp[2]], writes=[r_bp[3]])
                P.op("vector", lambda v: v.tensor_tensor(mx1, a4[:, :, 0:1], a4[:, :, 1:2],
                                                         ALU.max),
                     reads=[r_bp[2]], writes=[r_bp[3]])
                P.op("vector", lambda v: v.tensor_tensor(mn2, a4[:, :, 2:3], a4[:, :, 3:4],
                                                         ALU.min),
                     reads=[r_bp[2]], writes=[r_bp[3]])
                P.op("vector", lambda v: v.tensor_tensor(mx2, a4[:, :, 2:3], a4[:, :, 3:4],
                                                         ALU.max),
                     reads=[r_bp[2]], writes=[r_bp[3]])
                P.op("vector", lambda v: v.tensor_tensor(mn1, mn1, mn2, ALU.max),
                     reads=[r_bp[3]], writes=[r_bp[3]])
                P.op("vector", lambda v: v.tensor_tensor(mx1, mx1, mx2, ALU.min),
                     reads=[r_bp[3]], writes=[r_bp[3]])
                P.op("vector", lambda v: v.tensor_tensor(mn1, mn1, mx1, ALU.min),
                     reads=[r_bp[3]], writes=[r_bp[3]])
                m2b = bcast_last(m2t[:, 0:G].rearrange("p (g o) -> p g o", o=1), 4)
                cmpm = bpool[3][:, ICW:2 * ICW]
                P.op("vector", lambda v, m2b=m2b, cmpm=cmpm: v.tensor_tensor(
                    cmpm.rearrange("p (g k) -> p g k", k=4), a4, m2b, ALU.is_gt),
                    reads=[r_bp[2]], writes=[r_bp[3]])
                P.op("vector", lambda v, cmpm=cmpm, hsl=hsl: v.tensor_tensor(
                    hsl[:, 0:ICW], hsl[:, 0:ICW], cmpm, ALU.mult),
                     reads=[r_bp[3]], writes=[rht])
                pending.append((t, ic))
        while pending:
            drain(pending.pop(0))

    setup()
    load_square_weight(wq_d, wslotA, r_wA)
    load_square_weight(wv_d, wslotB, r_wB, cast_eng="scalar")
    lnab_bcast()
    carry_prologue()
    attn_qv_scan()
    attn_gate()
    handoff([r_fgc[0], r_fgc[1], r_fgc[2], r_fgc[3], r_fuc[0], r_fuc[1], r_fuc[2], r_fuc[3]],
            r_wA)
    for _q in range(4):
        load_q(0, _q)
    attn_out()
    handoff([r_fdc[0], r_fdc[1], r_fdc[2], r_fdc[3]] + r_stgF, r_wB)
    handoff([r_p0b[0], r_p0b[1]], [r_decay])
    ffn()
    P.barrier_wait("sync", out_toks)

    P.realize(nc, es)
    es.close()
    return nc


_NC = None


def get_nc():
    global _NC
    if _NC is None:
        _NC = build()
    return _NC


def kernel(L0_attn_x, L0_attn_Wq, L0_attn_Wv, L0_attn_Wg, L0_attn_Wo, L0_attn_beta,
           L0_attn_ln_g, L0_attn_ln_b, L0_ffn_w_gate, L0_ffn_w_up, L0_ffn_w_down):
    x = np.ascontiguousarray(np.asarray(L0_attn_x, dtype=np.float32))
    B, S, _ = x.shape
    half = S // 2
    beta = np.ascontiguousarray(L0_attn_beta, dtype=np.float32)
    weights = dict(
        Wq=np.ascontiguousarray(L0_attn_Wq, dtype=np.float32),
        Wv=np.ascontiguousarray(L0_attn_Wv, dtype=np.float32),
        Wg=np.ascontiguousarray(L0_attn_Wg, dtype=np.float32),
        Wo=np.ascontiguousarray(L0_attn_Wo, dtype=np.float32),
        w_gate=np.ascontiguousarray(L0_ffn_w_gate, dtype=np.float32),
        w_up=np.ascontiguousarray(L0_ffn_w_up, dtype=np.float32),
        w_down=np.ascontiguousarray(L0_ffn_w_down, dtype=np.float32),
        beta_r=beta.reshape(1, NH),
        beta_c=np.ascontiguousarray(beta.reshape(NH, 1)),
        ln_g=np.ascontiguousarray(np.asarray(L0_attn_ln_g, dtype=np.float32).reshape(1, D)),
        ln_b=np.ascontiguousarray(np.asarray(L0_attn_ln_b, dtype=np.float32).reshape(1, D)),
    )
    zeros = np.zeros((half, D), dtype=np.float32)
    in_maps = []
    for b in range(B):
        for hf in range(2):
            m = dict(weights)
            m["x"] = np.ascontiguousarray(x[b, hf * half:(hf + 1) * half])
            m["xprev"] = np.ascontiguousarray(x[b, 0:half]) if hf == 1 else zeros
            in_maps.append(m)
    nc = get_nc()
    res = run_bass_kernel_spmd(nc, in_maps, core_ids=list(range(8)))
    out = np.empty((B, S, D), dtype=np.float32)
    for i, r in enumerate(res.results):
        b, hf = divmod(i, 2)
        out[b, hf * half:(hf + 1) * half] = r["y"]
    return out


# version 26
# speedup vs baseline: 1.0401x; 1.0401x over previous
import numpy as np
from contextlib import ExitStack

import concourse.bass as bass
import concourse.mybir as mybir
from concourse.bass_utils import run_bass_kernel_spmd

F32 = mybir.dt.float32
BF16 = mybir.dt.bfloat16
AF = mybir.ActivationFunctionType
ALU = mybir.AluOpType

D = 2048
NH = 16
I = 8192
TPC = 2048
NT = 16
KC = 16
ICW = 1024
NIC = I // ICW
NQT = 4
QT = 4
EPS = 1e-5

ENGINES = ("tensor", "vector", "scalar", "gpsimd", "sync")


class Res:
    __slots__ = ("w", "r")

    def __init__(self):
        self.w = None
        self.r = []


def resarr(n):
    return [Res() for _ in range(n)]


class Prog:

    DMA_POOL = 10

    def __init__(self):
        self.items = {e: [] for e in ENGINES}
        self.cnt = {}
        self.marks = {e: {} for e in ENGINES}
        self.dma_rr = {e: 0 for e in ENGINES}

    def op(self, eng, fn, reads=(), writes=(), dma=False, extra_waits=()):
        toks = []
        for r in reads:
            if r.w is not None:
                toks.append(r.w)
        for w in writes:
            toks.extend(w.r)
            if w.w is not None:
                toks.append(w.w)
        toks.extend(extra_waits)
        need = {}
        for s, v in toks:
            if eng == "tensor" and s == "c_tensor":
                continue
            if v > need.get(s, -1):
                need[s] = v
        wm = self.marks[eng]
        waits = []
        for s, v in sorted(need.items()):
            if v > wm.get(s, 0):
                waits.append((s, v))
                wm[s] = v
        if dma:
            k = self.dma_rr[eng]
            self.dma_rr[eng] = (k + 1) % self.DMA_POOL
            sname = f"d_{eng}{k}"
            n = 16
            prev = self.cnt.get(sname, 0)
            if prev > wm.get(sname, 0):
                waits.append((sname, prev))
                wm[sname] = prev
        else:
            sname = "c_" + eng
            n = 1
        self.cnt[sname] = self.cnt.get(sname, 0) + n
        tok = (sname, self.cnt[sname])
        self.items[eng].append((waits, fn, sname, n))
        for r in reads:
            r.r.append(tok)
        for w in writes:
            w.w = tok
            w.r = []
        return tok

    def barrier_wait(self, eng, toks):
        need = {}
        for s, v in toks:
            if v > need.get(s, -1):
                need[s] = v
        wm = self.marks[eng]
        waits = []
        for s, v in sorted(need.items()):
            if v > wm.get(s, 0):
                waits.append((s, v))
                wm[s] = v
        self.items[eng].append((waits, None, None, 0))

    def realize(self, nc, es):
        sems = {name: es.enter_context(nc.semaphore(name)) for name in sorted(self.cnt)}
        blk = es.enter_context(nc.Block())
        for eng in ENGINES:
            ops = self.items[eng]
            if not ops:
                continue
            deco = getattr(blk, eng)

            def body(h, ops=ops):
                for waits, fn, sname, n in ops:
                    for s, v in waits:
                        h.wait_ge(sems[s], v)
                    if fn is not None:
                        ins = fn(h)
                        if sname is not None:
                            ins.then_inc(sems[sname], n)

            deco(body)


def handoff(new_res_list, old_res_list):
    toks = []
    for old in old_res_list:
        if old.w is not None:
            toks.append(old.w)
        toks.extend(old.r)
    for new in new_res_list:
        new.r = new.r + list(toks)


def bcast_last(ap, n):
    return bass.AP(ap.tensor, ap.offset, [list(d) for d in ap.ap[:-1]] + [[0, n]])


def build():
    nc = bass.Bass("TRN2", target_bir_lowering=False)

    x_d = nc.dram_tensor("x", [TPC, D], F32, kind="ExternalInput").ap()
    xprev_d = nc.dram_tensor("xprev", [TPC, D], F32, kind="ExternalInput").ap()
    wq_d = nc.dram_tensor("Wq", [D, D], F32, kind="ExternalInput").ap()
    wv_d = nc.dram_tensor("Wv", [D, D], F32, kind="ExternalInput").ap()
    wg_d = nc.dram_tensor("Wg", [D, D], F32, kind="ExternalInput").ap()
    wo_d = nc.dram_tensor("Wo", [D, D], F32, kind="ExternalInput").ap()
    fg_d = nc.dram_tensor("w_gate", [D, I], F32, kind="ExternalInput").ap()
    fu_d = nc.dram_tensor("w_up", [D, I], F32, kind="ExternalInput").ap()
    fd_d = nc.dram_tensor("w_down", [I, D], F32, kind="ExternalInput").ap()
    beta_r = nc.dram_tensor("beta_r", [1, NH], F32, kind="ExternalInput").ap()
    beta_c = nc.dram_tensor("beta_c", [NH, 1], F32, kind="ExternalInput").ap()
    lng_d = nc.dram_tensor("ln_g", [1, D], F32, kind="ExternalInput").ap()
    lnb_d = nc.dram_tensor("ln_b", [1, D], F32, kind="ExternalInput").ap()
    y_d = nc.dram_tensor("y", [TPC, D], F32, kind="ExternalOutput").ap()
    s_yaff = nc.dram_tensor("s_yaff", [NT, 128, D], BF16, kind="Internal").ap()
    s_ygT = nc.dram_tensor("s_ygT", [NT, 128, D], BF16, kind="Internal").ap()
    s_x2 = nc.dram_tensor("s_x2", [NT, 128, D], F32, kind="Internal").ap()
    s_xn3T = nc.dram_tensor("s_xn3T", [NT, 128, D], BF16, kind="Internal").ap()
    s_acc = nc.dram_tensor("s_acc", [NT, 128, D], F32, kind="Internal").ap()

    es = ExitStack()
    sb = lambda name, shape, dt: es.enter_context(nc.sbuf_tensor(name, shape, dt))
    psm = lambda name, shape, dt: es.enter_context(nc.psum_tensor(name, shape, dt))

    wslotA = sb("wslotA", [128, KC, D], BF16)
    wslotB = sb("wslotB", [128, KC, D], BF16)
    stg = [sb(f"stg{i}", [128, D], F32) for i in range(2)]
    fpool = [sb(f"fp{i}", [128, D], F32) for i in range(3)]
    bpool = [sb(f"bp{i}", [128, D], BF16) for i in range(4)]
    vst = sb("vst", [128, D], BF16)
    carry = sb("carry", [128, D], BF16)
    t_mat = sb("t_mat", [128, NH, 128], BF16)
    l3_mat = sb("l3_mat", [128, NH, 128], BF16)
    stats6 = [sb(f"stats6_{i}", [128, 24], F32) for i in range(2)]
    mv = [sb(f"mv{i}", [128, 2], F32) for i in range(2)]
    sd1 = [sb(f"sd1_{i}", [128, 1], F32) for i in range(2)]
    rstd = [sb(f"rstd{i}", [128, 1], F32) for i in range(2)]
    ln1m = sb("ln1m", [128, NT], F32)
    ln1r = sb("ln1r", [128, NT], F32)
    c_eps = sb("c_eps", [128, 1], F32)
    c_n87 = sb("c_n87", [128, 1], F32)
    c_nl128 = sb("c_nl128", [128, 1], F32)
    c_zero = sb("c_zero", [128, 1], F32)
    ones_row = sb("ones_row", [1, 128], F32)
    iota_e = sb("iota_e", [128, 128], F32)
    iota_m = sb("iota_m", [128, 128], F32)
    ident = sb("ident", [128, 128], F32)
    e127 = sb("e127", [128, 128], BF16)
    beta_row = sb("beta_row", [1, NH], F32)
    loglam_bc = sb("loglam_bc", [128, NH], F32)
    beta_col = sb("beta_col", [NH, 1], F32)
    loglam_col = sb("loglam_col", [NH, 1], F32)
    wlamT = sb("wlamT", [NH, 128], F32)
    xbarT = sb("xbarT", [128, KC, NH], BF16)
    psA = psm("psA", [128, D], F32)
    psB = psm("psB", [128, D], F32)

    lng_bc = stg[0][:].bitcast(BF16)[:, 0:D]
    lnb_bc = stg[0][:].bitcast(BF16)[:, D:2 * D]
    iota_t1 = iota_e
    fgc = wslotA[:, 0:8, :].rearrange("p a (b c) -> p (a b) c", c=ICW)
    fuc = wslotA[:, 8:16, :].rearrange("p a (b c) -> p (a b) c", c=ICW)
    fdc = wslotB[:, 0:8, :]
    stgF = [wslotB[:, 8 + 2 * k:10 + 2 * k, :].rearrange("p a b -> p (a b)")
            .bitcast(F32) for k in range(4)]
    p0buf = [t_mat[:].rearrange("p a b -> p (a b)"),
             l3_mat[:].rearrange("p a b -> p (a b)")]

    P = Prog()

    r_wA = resarr(KC)
    r_wB = resarr(KC)
    r_fgc = resarr(4)
    r_fuc = resarr(4)
    r_fdc = resarr(4)
    r_stgF = resarr(4)
    r_stg = resarr(2)
    r_fp = resarr(3)
    r_bp = resarr(6)
    r_p0b = resarr(2)
    r_stats = [Res(), Res()]
    r_ln1 = Res()
    r_const = Res()
    r_decay = Res()
    r_ident = Res()
    r_vst = Res()
    r_carry = Res()
    r_A = [Res(), Res()]
    r_B = [Res(), Res()]
    r_prol = Res()
    r_syaff = resarr(NT)
    r_sygT = resarr(NT)
    r_sx2 = resarr(NT)
    r_sxn3T = resarr(NT)
    r_sacc = resarr(NT)
    out_toks = []

    def setup():
        P.op("vector", lambda v: v.memset(c_eps[:], EPS), writes=[r_const])
        P.op("vector", lambda v: v.memset(c_n87[:], -87.0), writes=[r_const])
        P.op("vector", lambda v: v.memset(c_nl128[:], -float(np.log(128.0))), writes=[r_const])
        P.op("vector", lambda v: v.memset(c_zero[:], 0.0), writes=[r_const])
        P.op("vector", lambda v: v.memset(ones_row[:], 1.0), writes=[r_const])
        P.op("gpsimd", lambda g: g.iota(iota_e[:], pattern=[[1, 128]], base=0,
                                        channel_multiplier=-1,
                                        allow_small_or_imprecise_dtypes=True),
             writes=[r_ident])
        P.op("gpsimd", lambda g: g.affine_select(iota_m[:], iota_e[:], pattern=[[1, 128]],
                                                 base=0, channel_multiplier=-1,
                                                 compare_op=ALU.is_ge, fill=1e9),
             reads=[r_ident], writes=[r_ident])
        P.op("vector", lambda v: v.tensor_single_scalar(ident[:], iota_e[:], c_zero[:, 0:1],
                                                        ALU.is_equal),
             reads=[r_ident, r_const], writes=[r_ident])
        P.op("gpsimd", lambda g: g.iota(iota_t1[:], pattern=[[1, 128]], base=1,
                                        channel_multiplier=0,
                                        allow_small_or_imprecise_dtypes=True),
             reads=[r_ident], writes=[r_ident])
        P.op("vector", lambda v: v.memset(e127[:], 1.0), writes=[r_decay])
        P.op("gpsimd", lambda g: g.affine_select(e127[:], e127[:], pattern=[[0, 128]],
                                                 base=-127, channel_multiplier=1,
                                                 compare_op=ALU.is_ge, fill=0.0),
             reads=[r_decay], writes=[r_decay])
        P.op("sync", lambda s: s.dma_start(out=beta_row[:], in_=beta_r[:, :]), dma=True,
             writes=[r_decay])
        P.op("sync", lambda s: s.dma_start(out=beta_col[:], in_=beta_c[:, :]), dma=True,
             writes=[r_decay])
        P.op("scalar", lambda s: s.activation(beta_row[:], beta_row[:], AF.Sigmoid),
             reads=[r_decay], writes=[r_decay])
        P.op("scalar", lambda s: s.activation(beta_col[:], beta_col[:], AF.Sigmoid),
             reads=[r_decay], writes=[r_decay])
        P.op("scalar", lambda s: s.activation(beta_row[:], beta_row[:], AF.Ln),
             reads=[r_decay], writes=[r_decay])
        P.op("scalar", lambda s: s.activation(loglam_col[:], beta_col[:], AF.Ln),
             reads=[r_decay], writes=[r_decay])
        P.op("vector", lambda v: v.tensor_scalar_max(beta_row[:], beta_row[:], c_n87[0:1, 0:1]),
             reads=[r_const], writes=[r_decay])
        P.op("vector", lambda v: v.tensor_scalar_max(loglam_col[:], loglam_col[:], c_n87[0:NH, 0:1]),
             reads=[r_const], writes=[r_decay])
        P.op("tensor", lambda t: t.matmul(psA[:, 0:NH], ones_row[:], beta_row[:],
                                          start=True, stop=True),
             reads=[r_decay, r_const], writes=[r_A[0]])
        P.op("vector", lambda v: v.tensor_copy(loglam_bc[:], psA[:, 0:NH]),
             reads=[r_A[0]], writes=[r_decay, r_A[0]])
        for h in range(NH):
            P.op("scalar", lambda s, h=h: s.activation(t_mat[:, h, :], iota_m[:], AF.Exp,
                                                       scale=loglam_bc[:, h:h + 1]),
                 reads=[r_decay, r_ident], writes=[r_decay])
            P.op("scalar", lambda s, h=h: s.activation(l3_mat[:, h, :], iota_t1[:], AF.Exp,
                                                       scale=loglam_bc[:, h:h + 1],
                                                       bias=c_nl128[:, 0:1]),
                 reads=[r_decay, r_ident, r_const], writes=[r_decay])

    def load_square_weight(w_dram, slot_sb, slot_res, cast_eng=None):
        for kc in range(KC):
            for hh in range(2):
                load_half(slot_sb[:, kc, hh * 1024:(hh + 1) * 1024], slot_res[kc],
                          w_dram[kc * 128:(kc + 1) * 128, hh * 1024:(hh + 1) * 1024],
                          True)

    def ln_stats(src_ap, src_res, bank_tile=None, par=0):
        st6, mvp, sdp, rsp, rs = stats6[par], mv[par], sd1[par], rstd[par], r_stats[par]
        def f(v, src_ap=src_ap, st6=st6):
            ins = None
            for i in range(4):
                ins = v.bn_stats(st6[:, 6 * i:6 * i + 6], src_ap[:, 512 * i:512 * (i + 1)])
            return ins
        P.op("vector", f, reads=[src_res], writes=[rs])
        P.op("vector", lambda v, mvp=mvp, st6=st6: v.bn_aggr(mvp[:], st6[:]),
             reads=[rs], writes=[rs])
        P.op("scalar", lambda s, sdp=sdp, mvp=mvp: s.activation(sdp[:], mvp[:, 1:2], AF.Sqrt,
                                                                bias=c_eps[:, 0:1]),
             reads=[rs, r_const], writes=[rs])
        P.op("vector", lambda v, rsp=rsp, sdp=sdp: v.reciprocal(rsp[:], sdp[:]),
             reads=[rs], writes=[rs])
        if bank_tile is not None:
            P.op("vector", lambda v, t=bank_tile, mvp=mvp: v.tensor_copy(ln1m[:, t:t + 1],
                                                                         mvp[:, 0:1]),
                 reads=[rs], writes=[r_ln1])
            P.op("vector", lambda v, t=bank_tile, rsp=rsp: v.tensor_copy(ln1r[:, t:t + 1],
                                                                         rsp[:]),
                 reads=[rs], writes=[r_ln1])
        return mvp, rsp, rs

    def transpose_tile(src_ap, src_res, dst_sb, dst_res, evac_eng="scalar", width=D):
        nhalf = width // 1024
        for half in range(nhalf):
            def f(t, src_ap=src_ap, half=half):
                ins = None
                for j in range(8):
                    kc = half * 8 + j
                    ins = t.transpose(psA[:, half * 1024 + j * 128:half * 1024 + (j + 1) * 128],
                                      src_ap[:, kc * 128:(kc + 1) * 128], ident[:])
                return ins
            P.op("tensor", f, reads=[src_res, r_ident], writes=[r_A[half]])
            if evac_eng == "scalar":
                P.op("scalar", lambda s, dst_sb=dst_sb, half=half: s.copy(
                    dst_sb[:, half * 1024:(half + 1) * 1024],
                    psA[:, half * 1024:(half + 1) * 1024]),
                    reads=[r_A[half]], writes=[dst_res, r_A[half]])
            else:
                P.op("vector", lambda v, dst_sb=dst_sb, half=half: v.tensor_copy(
                    dst_sb[:, half * 1024:(half + 1) * 1024],
                    psA[:, half * 1024:(half + 1) * 1024]),
                    reads=[r_A[half]], writes=[dst_res, r_A[half]])

    def proj_mms(xT_ap, xT_res, w_sb, w_res, psum, psum_res, ns=(0, 4)):
        for kc in range(KC):
            def f(t, xT_ap=xT_ap, w_sb=w_sb, psum=psum, kc=kc, ns=ns):
                ins = None
                for n in range(ns[0], ns[1]):
                    ins = t.matmul(psum[:, n * 512:(n + 1) * 512], xT_ap[:, kc, :],
                                   w_sb[:, kc, n * 512:(n + 1) * 512],
                                   start=(kc == 0), stop=(kc == KC - 1))
                return ins
            P.op("tensor", f, reads=[xT_res, w_res[kc]], writes=psum_res)

    def lnab_bcast():
        P.op("sync", lambda s: s.dma_start(out=fpool[0][0:1, :], in_=lng_d[:, :]), dma=True,
             writes=[r_fp[0]])
        def bc_mm(tt, srcrow):
            ins = None
            for n in range(4):
                ins = tt.matmul(psA[:, n * 512:(n + 1) * 512], ones_row[:],
                                srcrow[:, n * 512:(n + 1) * 512], start=True, stop=True)
            return ins
        P.op("tensor", lambda t: bc_mm(t, fpool[0][0:1, :]),
             reads=[r_fp[0], r_const], writes=[r_A[0], r_A[1]])
        P.op("vector", lambda v: v.tensor_copy(lng_bc, psA[:]),
             reads=[r_A[0], r_A[1]], writes=[r_stg[0], r_A[0], r_A[1]])
        P.op("sync", lambda s: s.dma_start(out=fpool[0][0:1, :], in_=lnb_d[:, :]), dma=True,
             reads=[r_fp[0]], writes=[r_fp[0]])
        P.op("tensor", lambda t: bc_mm(t, fpool[0][0:1, :]),
             reads=[r_fp[0], r_const], writes=[r_A[0], r_A[1]])
        P.op("vector", lambda v: v.tensor_copy(lnb_bc, psA[:]),
             reads=[r_A[0], r_A[1]], writes=[r_stg[0], r_A[0], r_A[1], r_fp[0]])

    def carry_prologue():
        for t in range(NT):
            fx = t % 2
            P.op("sync", lambda s, t=t, fx=fx: s.dma_start(
                out=fpool[fx][:], in_=xprev_d[t * 128:(t + 1) * 128, :]),
                dma=True, writes=[r_fp[fx]])
            mvp, rsp, rs = ln_stats(fpool[fx][:], r_fp[fx], par=fx)
            P.op("vector", lambda v, fx=fx, mvp=mvp, rsp=rsp: v.tensor_scalar(
                fpool[2][:], fpool[fx][:], mvp[:, 0:1], rsp[:], ALU.subtract, ALU.mult),
                reads=[r_fp[fx], rs], writes=[r_fp[2], r_fp[fx]])
            P.op("vector", lambda v: v.tensor_copy(bpool[0][:], fpool[2][:]),
                 reads=[r_fp[2]], writes=[r_bp[0], r_fp[2]])
            P.op("gpsimd", lambda g, t=t: g.iota(wlamT[:], pattern=[[-1, 128]],
                                                 base=2047 - 128 * t, channel_multiplier=0,
                                                 allow_small_or_imprecise_dtypes=True),
                 writes=[r_prol])
            P.op("scalar", lambda s: s.activation(wlamT[:], wlamT[:], AF.Exp,
                                                  scale=loglam_col[:, 0:1]),
                 reads=[r_prol, r_decay], writes=[r_prol])
            P.op("tensor", lambda tt: tt.transpose(psB[:, 0:16], wlamT[:], ident[0:16, 0:16]),
                 reads=[r_prol, r_ident], writes=[r_B[0]])
            P.op("scalar", lambda s: s.copy(bpool[1][:, 0:16], psB[:, 0:16]),
                 reads=[r_B[0]], writes=[r_bp[1], r_B[0]])
            def f(tt, t=t):
                ins = None
                for n in range(4):
                    ins = tt.matmul(psA[0:NH, n * 512:(n + 1) * 512], bpool[1][:, 0:NH],
                                    bpool[0][:, n * 512:(n + 1) * 512],
                                    start=(t == 0), stop=(t == NT - 1))
                return ins
            P.op("tensor", f, reads=[r_bp[0], r_bp[1]], writes=[r_A[0], r_A[1]])
        P.op("scalar", lambda s: s.copy(fpool[2][0:NH, :], psA[0:NH, :]),
             reads=[r_A[0], r_A[1]], writes=[r_fp[2], r_A[0], r_A[1]])

        def trx(tt):
            ins = None
            for kc in range(KC):
                ins = tt.transpose(psB[:, kc * 16:(kc + 1) * 16],
                                   fpool[2][0:NH, kc * 128:(kc + 1) * 128], ident[0:16, 0:16])
            return ins
        P.op("tensor", trx, reads=[r_fp[2], r_ident], writes=[r_B[0]])
        P.op("scalar", lambda s: s.copy(xbarT[:].rearrange("p a b -> p (a b)"),
                                        psB[:, 0:KC * NH]),
             reads=[r_B[0]], writes=[r_prol, r_B[0], r_fp[2]])
        for h in range(NH):
            for kc in range(KC):
                sub = bpool[2][:, (kc % KC) * 128:(kc % KC + 1) * 128]
                src = xbarT[:, kc, h:h + 1]
                bc_ap = bcast_last(src, 128)
                P.op("vector", lambda v, sub=sub, bc_ap=bc_ap: v.tensor_copy(
                    sub.rearrange("p (a b) -> p a b", b=128), bc_ap),
                    reads=[r_prol], writes=[r_bp[2]])
                def f(tt, h=h, kc=kc, sub=sub):
                    return tt.matmul(psB[:, h * 128:(h + 1) * 128], sub,
                                     wslotB[:, kc, h * 128:(h + 1) * 128],
                                     start=(kc == 0), stop=(kc == KC - 1))
                P.op("tensor", f, reads=[r_bp[2], r_wB[kc]],
                     writes=[r_B[h // 8]])
        P.op("scalar", lambda s: s.copy(carry[:], psB[:]),
             reads=[r_B[0], r_B[1]], writes=[r_carry, r_B[0], r_B[1]])

    def attn_qv_scan():
        def prep(t):
            fx = t % 2
            P.op("sync", lambda s, t=t, fx=fx: s.dma_start(
                out=fpool[fx][:], in_=x_d[t * 128:(t + 1) * 128, :]), dma=True,
                writes=[r_fp[fx]])
            mvp, rsp, rs = ln_stats(fpool[fx][:], r_fp[fx], bank_tile=t, par=fx)
            P.op("vector", lambda v, fx=fx, mvp=mvp, rsp=rsp: v.tensor_scalar(
                fpool[2][:], fpool[fx][:], mvp[:, 0:1], rsp[:], ALU.subtract, ALU.mult),
                reads=[r_fp[fx], rs], writes=[r_fp[2]])

        prep(0)
        for t in range(NT):
            fx = t % 2
            bx = t % 2
            transpose_tile(fpool[2][:], r_fp[2], bpool[bx], r_bp[bx])
            xT = bpool[bx][:].rearrange("p (a b) -> p a b", b=128)
            proj_mms(xT, r_bp[bx], wslotA, r_wA, psB, [r_B[0], r_B[1]])
            proj_mms(xT, r_bp[bx], wslotB, r_wB, psA, [r_A[0], r_A[1]])
            if t + 1 < NT:
                prep(t + 1)
            P.op("scalar", lambda s: s.copy(vst[:], psA[:]),
                 reads=[r_A[0], r_A[1]], writes=[r_vst, r_A[0], r_A[1]])

            def scan_f(tt):
                ins = None
                for h in range(NH):
                    ins = tt.matmul(psA[:, h * 128:(h + 1) * 128], t_mat[:, h, :],
                                    vst[:, h * 128:(h + 1) * 128], start=True, stop=False)
                    ins = tt.matmul(psA[:, h * 128:(h + 1) * 128], l3_mat[:, h, :],
                                    carry[:, h * 128:(h + 1) * 128], start=False, stop=True)
                return ins
            P.op("tensor", scan_f, reads=[r_vst, r_carry, r_decay], writes=[r_A[0], r_A[1]])
            P.op("scalar", lambda s: s.copy(vst[:], psA[:]),
                 reads=[r_A[0], r_A[1]], writes=[r_vst])
            P.op("vector", lambda v: v.tensor_tensor(stg[1][:], psB[:], vst[:],
                                                     ALU.mult),
                 reads=[r_B[0], r_B[1], r_vst], writes=[r_stg[1], r_B[0], r_B[1]])
            if t + 1 < NT:
                def ext_f(tt):
                    ins = None
                    for n in range(4):
                        ins = tt.matmul(psA[:, n * 512:(n + 1) * 512], e127[:],
                                        vst[:, n * 512:(n + 1) * 512], start=True, stop=True)
                    return ins
                P.op("tensor", ext_f, reads=[r_vst, r_decay], writes=[r_A[0], r_A[1]])
                P.op("scalar", lambda s: s.copy(carry[:], psA[:]),
                     reads=[r_A[0], r_A[1]], writes=[r_carry, r_A[0], r_A[1]])
            mvp, rsp, rs = ln_stats(stg[1][:], r_stg[1], par=fx)
            P.op("vector", lambda v, mvp=mvp, rsp=rsp: v.tensor_scalar(
                stg[1][:], stg[1][:], mvp[:, 0:1], rsp[:],
                ALU.subtract, ALU.mult),
                 reads=[rs], writes=[r_stg[1]])
            P.op("vector", lambda v: v.tensor_tensor(stg[1][:], stg[1][:],
                                                     lng_bc, ALU.mult),
                 reads=[r_stg[0]], writes=[r_stg[1]])
            P.op("vector", lambda v: v.tensor_tensor(bpool[2][:], stg[1][:],
                                                     lnb_bc, ALU.add),
                 reads=[r_stg[0], r_stg[1]], writes=[r_bp[2], r_stg[1]])
            P.op("gpsimd", lambda g, t=t: g.dma_start(out=s_yaff[t], in_=bpool[2][:]),
                 dma=True, reads=[r_bp[2]], writes=[r_syaff[t]])

    def attn_gate():
        load_square_weight(wg_d, wslotA, r_wA)

        def drain_ygT(t):
            transpose_tile(fpool[t % 2][:], r_fp[t % 2], bpool[3], r_bp[3],
                           evac_eng="vector")
            P.op("gpsimd", lambda g, t=t: g.dma_start(out=s_ygT[t], in_=bpool[3][:]),
                 dma=True, reads=[r_bp[3]], writes=[r_sygT[t]])

        def prep(t):
            fx = t % 2
            P.op("sync", lambda s, t=t, fx=fx: s.dma_start(
                out=fpool[fx][:], in_=x_d[t * 128:(t + 1) * 128, :]), dma=True,
                writes=[r_fp[fx]])
            P.op("vector", lambda v, fx=fx, t=t: v.tensor_scalar(
                fpool[2][:], fpool[fx][:], ln1m[:, t:t + 1], ln1r[:, t:t + 1],
                ALU.subtract, ALU.mult),
                reads=[r_fp[fx], r_ln1], writes=[r_fp[2]])

        prep(0)
        pend = None
        for t in range(NT):
            fx = t % 2
            bx = t % 2
            transpose_tile(fpool[2][:], r_fp[2], bpool[bx], r_bp[bx])
            xT = bpool[bx][:].rearrange("p (a b) -> p a b", b=128)
            proj_mms(xT, r_bp[bx], wslotA, r_wA, psB, [r_B[0]], ns=(0, 2))
            P.op("scalar", lambda s, fx=fx: s.activation(fpool[fx][:, 0:1024],
                                                         psB[:, 0:1024], AF.Silu),
                 reads=[r_B[0]], writes=[r_fp[fx], r_B[0]])
            proj_mms(xT, r_bp[bx], wslotA, r_wA, psB, [r_B[1]], ns=(2, 4))
            P.op("scalar", lambda s, fx=fx: s.activation(fpool[fx][:, 1024:2048],
                                                         psB[:, 1024:2048], AF.Silu),
                 reads=[r_B[1]], writes=[r_fp[fx], r_B[1]])
            if pend is not None:
                drain_ygT(pend)
            if t + 1 < NT:
                prep(t + 1)
            P.op("gpsimd", lambda g, t=t: g.dma_start(out=bpool[2][:], in_=s_yaff[t]),
                 dma=True, reads=[r_syaff[t]], writes=[r_bp[2]])
            P.op("vector", lambda v, fx=fx: v.tensor_tensor(fpool[fx][:], bpool[2][:],
                                                            fpool[fx][:], ALU.mult),
                 reads=[r_bp[2]], writes=[r_fp[fx], r_bp[2]])
            pend = t
        drain_ygT(pend)

    def attn_out():
        load_square_weight(wo_d, wslotB, r_wB)

        def drain_xn3T(t):
            bx = t % 2
            pso, rpso = (psA, r_A) if t % 2 == 0 else (psB, r_B)
            for half in range(2):
                def ftr(tt, half=half, pso=pso):
                    ins = None
                    for j in range(8):
                        kc = half * 8 + j
                        ins = tt.transpose(
                            pso[:, half * 1024 + j * 128:half * 1024 + (j + 1) * 128],
                            fpool[2][:, kc * 128:(kc + 1) * 128], ident[:])
                    return ins
                P.op("tensor", ftr, reads=[r_fp[2], r_ident], writes=[rpso[half]])
                P.op("scalar", lambda s, half=half, pso=pso: s.copy(
                    bpool[2][:, half * 1024:(half + 1) * 1024],
                    pso[:, half * 1024:(half + 1) * 1024]),
                    reads=[rpso[half]], writes=[r_bp[2], rpso[half]])
            P.op("gpsimd", lambda g, t=t: g.dma_start(out=s_xn3T[t], in_=bpool[2][:]),
                 dma=True, reads=[r_bp[2]], writes=[r_sxn3T[t]])

        pend = None
        P.op("sync", lambda s: s.dma_start(out=bpool[0][:], in_=s_ygT[0]),
             dma=True, reads=[r_sygT[0]], writes=[r_bp[0]])
        for t in range(NT):
            bx = t % 2
            fx = t % 2
            ygT = bpool[bx][:].rearrange("p (a b) -> p a b", b=128)
            pso, rpso = (psA, r_A) if t % 2 == 0 else (psB, r_B)
            proj_mms(ygT, r_bp[bx], wslotB, r_wB, pso, [rpso[0], rpso[1]])
            if t + 1 < NT:
                P.op("sync", lambda s, t=t: s.dma_start(out=bpool[(t + 1) % 2][:],
                                                        in_=s_ygT[t + 1]),
                     dma=True, reads=[r_sygT[t + 1]], writes=[r_bp[(t + 1) % 2]])
            if pend is not None:
                drain_xn3T(pend)
            mvp, rsp, rs = ln_stats(pso[:], rpso[0], par=t % 2)
            P.op("vector", lambda v, pso=pso, mvp=mvp, rsp=rsp: v.tensor_scalar(
                fpool[2][:], pso[:], mvp[:, 0:1], rsp[:],
                ALU.subtract, ALU.mult),
                 reads=[rs, rpso[0], rpso[1]],
                 writes=[r_fp[2], rpso[0], rpso[1]])
            P.op("sync", lambda s, t=t, fx=fx: s.dma_start(
                out=fpool[fx][:], in_=x_d[t * 128:(t + 1) * 128, :]), dma=True,
                writes=[r_fp[fx]])
            P.op("vector", lambda v, fx=fx: v.tensor_tensor(fpool[fx][:], fpool[2][:],
                                                            fpool[fx][:], ALU.add),
                 reads=[r_fp[2]], writes=[r_fp[fx], r_fp[2]])
            P.op("gpsimd", lambda g, t=t, fx=fx: g.dma_start(out=s_x2[t], in_=fpool[fx][:]),
                 dma=True, reads=[r_fp[fx]], writes=[r_sx2[t]])
            mvp, rsp, rs = ln_stats(fpool[fx][:], r_fp[fx], par=t % 2)
            P.op("vector", lambda v, fx=fx, mvp=mvp, rsp=rsp: v.tensor_scalar(
                fpool[2][:], fpool[fx][:], mvp[:, 0:1], rsp[:], ALU.subtract, ALU.mult),
                reads=[r_fp[fx], rs], writes=[r_fp[2], r_fp[fx]])
            pend = t
        drain_xn3T(pend)

    wstg4 = [stg[0][:][:, 0:1024], stg[0][:][:, 1024:2048],
             stg[1][:][:, 0:1024], stg[1][:][:, 1024:2048]]
    r_wstg4 = resarr(4)
    wrr = [0]

    def load_half(dst_ap, dst_res, src_ap, couple):
        k = wrr[0] % 4
        wrr[0] += 1
        extra = [r_stg[k // 2]] if couple else []
        P.op("sync", lambda s, src_ap=src_ap, k=k: s.dma_start(
            out=wstg4[k], in_=src_ap), dma=True, writes=[r_wstg4[k]] + extra)
        if wrr[0] % 2 == 0:
            P.op("vector", lambda v, dst_ap=dst_ap, k=k: v.tensor_copy(
                dst_ap, wstg4[k]),
                reads=[r_wstg4[k]], writes=[dst_res, r_wstg4[k]])
        else:
            P.op("scalar", lambda s, dst_ap=dst_ap, k=k: s.copy(
                dst_ap, wstg4[k]),
                reads=[r_wstg4[k]], writes=[dst_res, r_wstg4[k]])

    def load_cast(dst_ap, dst_res, src_ap, width):
        for hh in range(width // 1024):
            load_half(dst_ap[:, hh * 1024:(hh + 1) * 1024], dst_res,
                      src_ap[:, hh * 1024:(hh + 1) * 1024], False)

    def load_q(ic, q):
        for kc in range(q * 4, (q + 1) * 4):
            load_cast(fgc[:, kc, :], r_fgc[q],
                      fg_d[kc * 128:(kc + 1) * 128, ic * ICW:(ic + 1) * ICW], ICW)
            load_cast(fuc[:, kc, :], r_fuc[q],
                      fu_d[kc * 128:(kc + 1) * 128, ic * ICW:(ic + 1) * ICW], ICW)

    def load_fdc(ic, cs, ce):
        for c in range(cs, ce):
            load_cast(fdc[:, c, :], r_fdc[c // 2],
                      fd_d[(ic * ICW + c * 128):(ic * ICW + (c + 1) * 128), :], D)

    def ffn():
        hbuf = [p0buf[0].bitcast(F32), p0buf[1].bitcast(F32),
                fpool[2][:][:, 0:ICW]]
        hflats = [p0buf[0], p0buf[1], fpool[2][:].bitcast(BF16)[:, 0:2 * ICW]]
        r_h = [r_p0b[0], r_p0b[1], r_fp[2]]
        drr = [0]

        def emit_ln4(t):
            P.op("sync", lambda s, t=t: s.dma_start(out=fpool[2][:], in_=s_acc[t]),
                 dma=True, reads=[r_sacc[t]], writes=[r_fp[2]])
            mvp, rsp, rs = ln_stats(fpool[2][:], r_fp[2], par=t % 2)
            fx = t % 2
            P.op("sync", lambda s, t=t, fx=fx: s.dma_start(out=fpool[fx][:], in_=s_x2[t]),
                 dma=True, reads=[r_sx2[t]], writes=[r_fp[fx]])
            P.op("vector", lambda v, fx=fx, mvp=mvp, rsp=rsp: v.tensor_scalar(
                fpool[2][:], fpool[2][:], mvp[:, 0:1], rsp[:],
                ALU.subtract, ALU.mult),
                 reads=[rs], writes=[r_fp[2]])
            P.op("vector", lambda v, fx=fx: v.tensor_tensor(fpool[2][:], fpool[2][:],
                                                            fpool[fx][:], ALU.add),
                 reads=[r_fp[fx]], writes=[r_fp[2], r_fp[fx]])
            yres = Res()
            tok = P.op("sync", lambda s, t=t: s.dma_start(
                out=y_d[t * 128:(t + 1) * 128, :], in_=fpool[2][:]),
                dma=True, reads=[r_fp[2]], writes=[yres])
            out_toks.append(tok)

        def drain(pend):
            pt, pic = pend
            par = ((pic * NT + pt) % 3) if pic < NIC - 1 else (pt % 2)
            hsp = hbuf[par]
            hflat = hflats[par]
            rh = r_h[par]

            def trf(tt, hsp=hsp):
                ins = None
                for j in range(8):
                    ins = tt.transpose(psA[:, j * 128:(j + 1) * 128],
                                       hsp[:, j * 128:(j + 1) * 128], ident[:])
                return ins
            P.op("tensor", trf, reads=[rh, r_ident], writes=[r_A[0]])
            hT_flat = hflat
            rc = [Res(), Res()]
            for hh in range(2):
                P.op("scalar", lambda s, hT_flat=hT_flat, hh=hh: s.copy(
                    hT_flat[:, hh * 512:(hh + 1) * 512],
                    psA[:, hh * 512:(hh + 1) * 512]),
                     reads=[r_A[0]], writes=[rh, rc[hh], r_A[0]])
            hT = hT_flat[:, 0:ICW].rearrange("p (a b) -> p a b", b=128)
            for q in range(4):
                def f(tt, q=q, hT=hT):
                    ins = None
                    for kc in range(q * 2, (q + 1) * 2):
                        for n in range(4):
                            ins = tt.matmul(psB[:, n * 512:(n + 1) * 512], hT[:, kc, :],
                                            fdc[:, kc, n * 512:(n + 1) * 512],
                                            start=(kc == 0), stop=(kc == ICW // 128 - 1))
                    return ins
                P.op("tensor", f, reads=[rh, rc[q // 2], r_fdc[q]],
                     writes=[r_B[0], r_B[1]])
            dk = drr[0] % 4
            drr[0] += 1
            ev = stgF[dk]
            P.op("vector", lambda v, ev=ev: v.tensor_copy(ev[:], psB[:]),
                 reads=[r_B[0], r_B[1]], writes=[r_stgF[dk], r_B[0], r_B[1]])
            if pic == 0:
                P.op("gpsimd", lambda g, pt=pt, ev=ev: g.dma_start(
                    out=s_acc[pt], in_=ev[:]),
                    dma=True, reads=[r_stgF[dk]], writes=[r_sacc[pt]])
            else:
                P.op("gpsimd", lambda g, pt=pt, ev=ev: g.dma_start(
                    out=s_acc[pt], in_=ev[:], accum_op=ALU.add),
                    dma=True, reads=[r_stgF[dk]], writes=[r_sacc[pt]])
            if pic == NIC - 1:
                emit_ln4(pt)

        pending = []
        for ic in range(NIC):
            while pending:
                drain(pending.pop(0))
            for t in range(NT):
                bx = t % 2
                fx = t % 2
                P.op("sync", lambda s, t=t, bx=bx: s.dma_start(out=bpool[bx][:],
                                                               in_=s_xn3T[t]),
                     dma=True, reads=[r_sxn3T[t]], writes=[r_bp[bx]])
                xT = bpool[bx][:].rearrange("p (a b) -> p a b", b=128)
                for q in range(4):
                    def f(tt, xT=xT, q=q):
                        ins = None
                        for kc in range(q * 4, (q + 1) * 4):
                            for n in range(2):
                                ins = tt.matmul(psA[:, n * 512:(n + 1) * 512], xT[:, kc, :],
                                                fgc[:, kc, n * 512:(n + 1) * 512],
                                                start=(kc == 0), stop=(kc == KC - 1))
                        return ins
                    P.op("tensor", f, reads=[r_bp[bx], r_fgc[q]], writes=[r_A[0]])
                P.op("scalar", lambda s, fx=fx: s.activation(fpool[fx][:, 0:ICW],
                                                             psA[:, 0:ICW], AF.Silu),
                     reads=[r_A[0]], writes=[r_fp[fx], r_A[0]])
                for q in range(4):
                    def f(tt, xT=xT, q=q):
                        ins = None
                        for kc in range(q * 4, (q + 1) * 4):
                            for n in range(2):
                                ins = tt.matmul(psA[:, 1024 + n * 512:1024 + (n + 1) * 512],
                                                xT[:, kc, :], fuc[:, kc, n * 512:(n + 1) * 512],
                                                start=(kc == 0), stop=(kc == KC - 1))
                        return ins
                    P.op("tensor", f, reads=[r_bp[bx], r_fuc[q]], writes=[r_A[1]])
                    if t == NT - 1 and ic + 1 < NIC:
                        load_q(ic + 1, q)
                if t < 2:
                    load_fdc(ic, t * 4, (t + 1) * 4)
                hpar = ((ic * NT + t) % 3) if ic < NIC - 1 else (t % 2)
                hsl = hbuf[hpar]
                rht = r_h[hpar]
                P.op("vector", lambda v, fx=fx, hsl=hsl: v.tensor_tensor(
                    hsl[:, 0:ICW], psA[:, ICW:2 * ICW],
                    fpool[fx][:, 0:ICW], ALU.mult),
                     reads=[r_A[1], r_fp[fx]], writes=[rht, r_A[1], r_fp[fx]])
                maxp = 2 if ic < NIC - 1 else 1
                if len(pending) >= maxp:
                    drain(pending.pop(0))
                habs = bpool[2]
                P.op("scalar", lambda s, hsl=hsl: s.activation(habs[:, 0:ICW],
                                                               hsl[:, 0:ICW], AF.Abs),
                     reads=[rht], writes=[r_bp[2]])
                a4 = habs[:, 0:ICW].rearrange("p (g k) -> p g k", k=4)
                G = ICW // 4
                m2t = bpool[3]
                mn1 = m2t[:, 0:G].rearrange("p (g o) -> p g o", o=1)
                mx1 = m2t[:, G:2 * G].rearrange("p (g o) -> p g o", o=1)
                mn2 = m2t[:, 2 * G:3 * G].rearrange("p (g o) -> p g o", o=1)
                mx2 = m2t[:, 3 * G:4 * G].rearrange("p (g o) -> p g o", o=1)
                P.op("vector", lambda v: v.tensor_tensor(mn1, a4[:, :, 0:1], a4[:, :, 1:2],
                                                         ALU.min),
                     reads=[r_bp[2]], writes=[r_bp[3]])
                P.op("vector", lambda v: v.tensor_tensor(mx1, a4[:, :, 0:1], a4[:, :, 1:2],
                                                         ALU.max),
                     reads=[r_bp[2]], writes=[r_bp[3]])
                P.op("vector", lambda v: v.tensor_tensor(mn2, a4[:, :, 2:3], a4[:, :, 3:4],
                                                         ALU.min),
                     reads=[r_bp[2]], writes=[r_bp[3]])
                P.op("vector", lambda v: v.tensor_tensor(mx2, a4[:, :, 2:3], a4[:, :, 3:4],
                                                         ALU.max),
                     reads=[r_bp[2]], writes=[r_bp[3]])
                P.op("vector", lambda v: v.tensor_tensor(mn1, mn1, mn2, ALU.max),
                     reads=[r_bp[3]], writes=[r_bp[3]])
                P.op("vector", lambda v: v.tensor_tensor(mx1, mx1, mx2, ALU.min),
                     reads=[r_bp[3]], writes=[r_bp[3]])
                P.op("vector", lambda v: v.tensor_tensor(mn1, mn1, mx1, ALU.min),
                     reads=[r_bp[3]], writes=[r_bp[3]])
                m2b = bcast_last(m2t[:, 0:G].rearrange("p (g o) -> p g o", o=1), 4)
                cmpm = bpool[3][:, ICW:2 * ICW]
                P.op("vector", lambda v, m2b=m2b, cmpm=cmpm: v.tensor_tensor(
                    cmpm.rearrange("p (g k) -> p g k", k=4), a4, m2b, ALU.is_gt),
                    reads=[r_bp[2]], writes=[r_bp[3]])
                P.op("vector", lambda v, cmpm=cmpm, hsl=hsl: v.tensor_tensor(
                    hsl[:, 0:ICW], hsl[:, 0:ICW], cmpm, ALU.mult),
                     reads=[r_bp[3]], writes=[rht])
                pending.append((t, ic))
        while pending:
            drain(pending.pop(0))

    setup()
    load_square_weight(wq_d, wslotA, r_wA)
    load_square_weight(wv_d, wslotB, r_wB, cast_eng="scalar")
    lnab_bcast()
    carry_prologue()
    attn_qv_scan()
    attn_gate()
    handoff([r_fgc[0], r_fgc[1], r_fgc[2], r_fgc[3], r_fuc[0], r_fuc[1], r_fuc[2], r_fuc[3]],
            r_wA)
    for _q in range(4):
        load_q(0, _q)
    attn_out()
    handoff([r_fdc[0], r_fdc[1], r_fdc[2], r_fdc[3]] + r_stgF, r_wB)
    handoff([r_p0b[0], r_p0b[1]], [r_decay])
    ffn()
    P.barrier_wait("sync", out_toks)

    P.realize(nc, es)
    es.close()
    return nc


_NC = None


def get_nc():
    global _NC
    if _NC is None:
        _NC = build()
    return _NC


def kernel(L0_attn_x, L0_attn_Wq, L0_attn_Wv, L0_attn_Wg, L0_attn_Wo, L0_attn_beta,
           L0_attn_ln_g, L0_attn_ln_b, L0_ffn_w_gate, L0_ffn_w_up, L0_ffn_w_down):
    x = np.ascontiguousarray(np.asarray(L0_attn_x, dtype=np.float32))
    B, S, _ = x.shape
    half = S // 2
    beta = np.ascontiguousarray(L0_attn_beta, dtype=np.float32)
    weights = dict(
        Wq=np.ascontiguousarray(L0_attn_Wq, dtype=np.float32),
        Wv=np.ascontiguousarray(L0_attn_Wv, dtype=np.float32),
        Wg=np.ascontiguousarray(L0_attn_Wg, dtype=np.float32),
        Wo=np.ascontiguousarray(L0_attn_Wo, dtype=np.float32),
        w_gate=np.ascontiguousarray(L0_ffn_w_gate, dtype=np.float32),
        w_up=np.ascontiguousarray(L0_ffn_w_up, dtype=np.float32),
        w_down=np.ascontiguousarray(L0_ffn_w_down, dtype=np.float32),
        beta_r=beta.reshape(1, NH),
        beta_c=np.ascontiguousarray(beta.reshape(NH, 1)),
        ln_g=np.ascontiguousarray(np.asarray(L0_attn_ln_g, dtype=np.float32).reshape(1, D)),
        ln_b=np.ascontiguousarray(np.asarray(L0_attn_ln_b, dtype=np.float32).reshape(1, D)),
    )
    zeros = np.zeros((half, D), dtype=np.float32)
    in_maps = []
    for b in range(B):
        for hf in range(2):
            m = dict(weights)
            m["x"] = np.ascontiguousarray(x[b, hf * half:(hf + 1) * half])
            m["xprev"] = np.ascontiguousarray(x[b, 0:half]) if hf == 1 else zeros
            in_maps.append(m)
    nc = get_nc()
    res = run_bass_kernel_spmd(nc, in_maps, core_ids=list(range(8)))
    out = np.empty((B, S, D), dtype=np.float32)
    for i, r in enumerate(res.results):
        b, hf = divmod(i, 2)
        out[b, hf * half:(hf + 1) * half] = r["y"]
    return out


# version 27 (final)
# speedup vs baseline: 1.1878x; 1.1420x over previous
import numpy as np
from contextlib import ExitStack

import concourse.bass as bass
import concourse.mybir as mybir
from concourse.bass_utils import run_bass_kernel_spmd

F32 = mybir.dt.float32
BF16 = mybir.dt.bfloat16
AF = mybir.ActivationFunctionType
ALU = mybir.AluOpType

D = 2048
NH = 16
I = 8192
TPC = 2048
NT = 16
KC = 16
ICW = 1024
NIC = I // ICW
NQT = 4
QT = 4
EPS = 1e-5

ENGINES = ("tensor", "vector", "scalar", "gpsimd", "sync")


class Res:
    __slots__ = ("w", "r")

    def __init__(self):
        self.w = None
        self.r = []


def resarr(n):
    return [Res() for _ in range(n)]


class Prog:

    DMA_POOL = 10

    def __init__(self):
        self.items = {e: [] for e in ENGINES}
        self.cnt = {}
        self.marks = {e: {} for e in ENGINES}
        self.dma_rr = {e: 0 for e in ENGINES}

    def op(self, eng, fn, reads=(), writes=(), dma=False, extra_waits=()):
        toks = []
        for r in reads:
            if r.w is not None:
                toks.append(r.w)
        for w in writes:
            toks.extend(w.r)
            if w.w is not None:
                toks.append(w.w)
        toks.extend(extra_waits)
        need = {}
        for s, v in toks:
            if eng == "tensor" and s == "c_tensor":
                continue
            if v > need.get(s, -1):
                need[s] = v
        wm = self.marks[eng]
        waits = []
        for s, v in sorted(need.items()):
            if v > wm.get(s, 0):
                waits.append((s, v))
                wm[s] = v
        if dma:
            k = self.dma_rr[eng]
            self.dma_rr[eng] = (k + 1) % self.DMA_POOL
            sname = f"d_{eng}{k}"
            n = 16
            prev = self.cnt.get(sname, 0)
            if prev > wm.get(sname, 0):
                waits.append((sname, prev))
                wm[sname] = prev
        else:
            sname = "c_" + eng
            n = 1
        self.cnt[sname] = self.cnt.get(sname, 0) + n
        tok = (sname, self.cnt[sname])
        self.items[eng].append((waits, fn, sname, n))
        for r in reads:
            r.r.append(tok)
        for w in writes:
            w.w = tok
            w.r = []
        return tok

    def barrier_wait(self, eng, toks):
        need = {}
        for s, v in toks:
            if v > need.get(s, -1):
                need[s] = v
        wm = self.marks[eng]
        waits = []
        for s, v in sorted(need.items()):
            if v > wm.get(s, 0):
                waits.append((s, v))
                wm[s] = v
        self.items[eng].append((waits, None, None, 0))

    def realize(self, nc, es):
        sems = {name: es.enter_context(nc.semaphore(name)) for name in sorted(self.cnt)}
        blk = es.enter_context(nc.Block())
        for eng in ENGINES:
            ops = self.items[eng]
            if not ops:
                continue
            deco = getattr(blk, eng)

            def body(h, ops=ops):
                for waits, fn, sname, n in ops:
                    for s, v in waits:
                        h.wait_ge(sems[s], v)
                    if fn is not None:
                        ins = fn(h)
                        if sname is not None:
                            ins.then_inc(sems[sname], n)

            deco(body)


def handoff(new_res_list, old_res_list):
    toks = []
    for old in old_res_list:
        if old.w is not None:
            toks.append(old.w)
        toks.extend(old.r)
    for new in new_res_list:
        new.r = new.r + list(toks)


def bcast_last(ap, n):
    return bass.AP(ap.tensor, ap.offset, [list(d) for d in ap.ap[:-1]] + [[0, n]])


def build():
    nc = bass.Bass("TRN2", target_bir_lowering=False)

    x_d = nc.dram_tensor("x", [TPC, D], F32, kind="ExternalInput").ap()
    xprev_d = nc.dram_tensor("xprev", [TPC, D], F32, kind="ExternalInput").ap()
    wq_d = nc.dram_tensor("Wq", [D, D], F32, kind="ExternalInput").ap()
    wv_d = nc.dram_tensor("Wv", [D, D], F32, kind="ExternalInput").ap()
    wg_d = nc.dram_tensor("Wg", [D, D], F32, kind="ExternalInput").ap()
    wo_d = nc.dram_tensor("Wo", [D, D], F32, kind="ExternalInput").ap()
    fg_d = nc.dram_tensor("w_gate", [D, I], F32, kind="ExternalInput").ap()
    fu_d = nc.dram_tensor("w_up", [D, I], F32, kind="ExternalInput").ap()
    fd_d = nc.dram_tensor("w_down", [I, D], F32, kind="ExternalInput").ap()
    beta_r = nc.dram_tensor("beta_r", [1, NH], F32, kind="ExternalInput").ap()
    beta_c = nc.dram_tensor("beta_c", [NH, 1], F32, kind="ExternalInput").ap()
    lng_d = nc.dram_tensor("ln_g", [1, D], F32, kind="ExternalInput").ap()
    lnb_d = nc.dram_tensor("ln_b", [1, D], F32, kind="ExternalInput").ap()
    y_d = nc.dram_tensor("y", [TPC, D], F32, kind="ExternalOutput").ap()
    s_yaff = nc.dram_tensor("s_yaff", [NT, 128, D], BF16, kind="Internal").ap()
    s_ygT = nc.dram_tensor("s_ygT", [NT, 128, D], BF16, kind="Internal").ap()
    s_x2 = nc.dram_tensor("s_x2", [NT, 128, D], F32, kind="Internal").ap()
    s_xn3T = nc.dram_tensor("s_xn3T", [NT, 128, D], BF16, kind="Internal").ap()
    s_acc = nc.dram_tensor("s_acc", [NT, 128, D], F32, kind="Internal").ap()

    es = ExitStack()
    sb = lambda name, shape, dt: es.enter_context(nc.sbuf_tensor(name, shape, dt))
    psm = lambda name, shape, dt: es.enter_context(nc.psum_tensor(name, shape, dt))

    wslotA = sb("wslotA", [128, KC, D], BF16)
    wslotB = sb("wslotB", [128, KC, D], BF16)
    stg = [sb(f"stg{i}", [128, D], F32) for i in range(2)]
    fpool = [sb(f"fp{i}", [128, D], F32) for i in range(3)]
    bpool = [sb(f"bp{i}", [128, D], BF16) for i in range(4)]
    vst = sb("vst", [128, D], BF16)
    carry = sb("carry", [128, D], BF16)
    t_mat = sb("t_mat", [128, NH, 128], BF16)
    l3_mat = sb("l3_mat", [128, NH, 128], BF16)
    stats6 = [sb(f"stats6_{i}", [128, 24], F32) for i in range(2)]
    mv = [sb(f"mv{i}", [128, 2], F32) for i in range(2)]
    sd1 = [sb(f"sd1_{i}", [128, 1], F32) for i in range(2)]
    rstd = [sb(f"rstd{i}", [128, 1], F32) for i in range(2)]
    ln1m = sb("ln1m", [128, NT], F32)
    ln1r = sb("ln1r", [128, NT], F32)
    c_eps = sb("c_eps", [128, 1], F32)
    c_n87 = sb("c_n87", [128, 1], F32)
    c_nl128 = sb("c_nl128", [128, 1], F32)
    c_zero = sb("c_zero", [128, 1], F32)
    ones_row = sb("ones_row", [1, 128], F32)
    iota_e = sb("iota_e", [128, 128], F32)
    iota_m = sb("iota_m", [128, 128], F32)
    ident = sb("ident", [128, 128], F32)
    e127 = sb("e127", [128, 128], BF16)
    beta_row = sb("beta_row", [1, NH], F32)
    loglam_bc = sb("loglam_bc", [128, NH], F32)
    beta_col = sb("beta_col", [NH, 1], F32)
    loglam_col = sb("loglam_col", [NH, 1], F32)
    wlamT = sb("wlamT", [NH, 128], F32)
    xbarT = sb("xbarT", [128, KC, NH], BF16)
    psA = psm("psA", [128, D], F32)
    psB = psm("psB", [128, D], F32)

    lng_bc = stg[0][:].bitcast(BF16)[:, 0:D]
    lnb_bc = stg[0][:].bitcast(BF16)[:, D:2 * D]
    iota_t1 = iota_e
    fgc = wslotA[:, 0:8, :].rearrange("p a (b c) -> p (a b) c", c=ICW)
    fuc = wslotA[:, 8:16, :].rearrange("p a (b c) -> p (a b) c", c=ICW)
    fdc = wslotB[:, 0:8, :]
    stgF = [wslotB[:, 8 + 2 * k:10 + 2 * k, :].rearrange("p a b -> p (a b)")
            .bitcast(F32) for k in range(4)]
    p0buf = [t_mat[:].rearrange("p a b -> p (a b)"),
             l3_mat[:].rearrange("p a b -> p (a b)")]

    P = Prog()

    r_wA = resarr(KC)
    r_wB = resarr(KC)
    r_fgc = resarr(4)
    r_fuc = resarr(4)
    r_fdc = resarr(4)
    r_stgF = resarr(4)
    r_stg = resarr(2)
    r_fp = resarr(3)
    r_bp = resarr(6)
    r_p0b = resarr(2)
    r_stats = [Res(), Res()]
    r_ln1 = Res()
    r_const = Res()
    r_decay = Res()
    r_ident = Res()
    r_vst = Res()
    r_carry = Res()
    r_A = [Res(), Res()]
    r_B = [Res(), Res()]
    r_prol = Res()
    r_syaff = resarr(NT)
    r_sygT = resarr(NT)
    r_sx2 = resarr(NT)
    r_sxn3T = resarr(NT)
    r_sacc = resarr(NT)
    out_toks = []

    def setup():
        P.op("vector", lambda v: v.memset(c_eps[:], EPS), writes=[r_const])
        P.op("vector", lambda v: v.memset(c_n87[:], -87.0), writes=[r_const])
        P.op("vector", lambda v: v.memset(c_nl128[:], -float(np.log(128.0))), writes=[r_const])
        P.op("vector", lambda v: v.memset(c_zero[:], 0.0), writes=[r_const])
        P.op("vector", lambda v: v.memset(ones_row[:], 1.0), writes=[r_const])
        P.op("gpsimd", lambda g: g.iota(iota_e[:], pattern=[[1, 128]], base=0,
                                        channel_multiplier=-1,
                                        allow_small_or_imprecise_dtypes=True),
             writes=[r_ident])
        P.op("gpsimd", lambda g: g.affine_select(iota_m[:], iota_e[:], pattern=[[1, 128]],
                                                 base=0, channel_multiplier=-1,
                                                 compare_op=ALU.is_ge, fill=1e9),
             reads=[r_ident], writes=[r_ident])
        P.op("vector", lambda v: v.tensor_single_scalar(ident[:], iota_e[:], c_zero[:, 0:1],
                                                        ALU.is_equal),
             reads=[r_ident, r_const], writes=[r_ident])
        P.op("gpsimd", lambda g: g.iota(iota_t1[:], pattern=[[1, 128]], base=1,
                                        channel_multiplier=0,
                                        allow_small_or_imprecise_dtypes=True),
             reads=[r_ident], writes=[r_ident])
        P.op("vector", lambda v: v.memset(e127[:], 1.0), writes=[r_decay])
        P.op("gpsimd", lambda g: g.affine_select(e127[:], e127[:], pattern=[[0, 128]],
                                                 base=-127, channel_multiplier=1,
                                                 compare_op=ALU.is_ge, fill=0.0),
             reads=[r_decay], writes=[r_decay])
        P.op("sync", lambda s: s.dma_start(out=beta_row[:], in_=beta_r[:, :]), dma=True,
             writes=[r_decay])
        P.op("sync", lambda s: s.dma_start(out=beta_col[:], in_=beta_c[:, :]), dma=True,
             writes=[r_decay])
        P.op("scalar", lambda s: s.activation(beta_row[:], beta_row[:], AF.Sigmoid),
             reads=[r_decay], writes=[r_decay])
        P.op("scalar", lambda s: s.activation(beta_col[:], beta_col[:], AF.Sigmoid),
             reads=[r_decay], writes=[r_decay])
        P.op("scalar", lambda s: s.activation(beta_row[:], beta_row[:], AF.Ln),
             reads=[r_decay], writes=[r_decay])
        P.op("scalar", lambda s: s.activation(loglam_col[:], beta_col[:], AF.Ln),
             reads=[r_decay], writes=[r_decay])
        P.op("vector", lambda v: v.tensor_scalar_max(beta_row[:], beta_row[:], c_n87[0:1, 0:1]),
             reads=[r_const], writes=[r_decay])
        P.op("vector", lambda v: v.tensor_scalar_max(loglam_col[:], loglam_col[:], c_n87[0:NH, 0:1]),
             reads=[r_const], writes=[r_decay])
        P.op("tensor", lambda t: t.matmul(psA[:, 0:NH], ones_row[:], beta_row[:],
                                          start=True, stop=True),
             reads=[r_decay, r_const], writes=[r_A[0]])
        P.op("vector", lambda v: v.tensor_copy(loglam_bc[:], psA[:, 0:NH]),
             reads=[r_A[0]], writes=[r_decay, r_A[0]])
        for h in range(NH):
            P.op("scalar", lambda s, h=h: s.activation(t_mat[:, h, :], iota_m[:], AF.Exp,
                                                       scale=loglam_bc[:, h:h + 1]),
                 reads=[r_decay, r_ident], writes=[r_decay])
            P.op("scalar", lambda s, h=h: s.activation(l3_mat[:, h, :], iota_t1[:], AF.Exp,
                                                       scale=loglam_bc[:, h:h + 1],
                                                       bias=c_nl128[:, 0:1]),
                 reads=[r_decay, r_ident, r_const], writes=[r_decay])

    def load_square_weight(w_dram, slot_sb, slot_res, cast_eng=None):
        for kc in range(KC):
            for hh in range(2):
                load_half(slot_sb[:, kc, hh * 1024:(hh + 1) * 1024], slot_res[kc],
                          w_dram[kc * 128:(kc + 1) * 128, hh * 1024:(hh + 1) * 1024],
                          True)

    def ln_stats(src_ap, src_res, bank_tile=None, par=0):
        st6, mvp, sdp, rsp, rs = stats6[par], mv[par], sd1[par], rstd[par], r_stats[par]
        def f(v, src_ap=src_ap, st6=st6):
            ins = None
            for i in range(4):
                ins = v.bn_stats(st6[:, 6 * i:6 * i + 6], src_ap[:, 512 * i:512 * (i + 1)])
            return ins
        P.op("vector", f, reads=[src_res], writes=[rs])
        P.op("vector", lambda v, mvp=mvp, st6=st6: v.bn_aggr(mvp[:], st6[:]),
             reads=[rs], writes=[rs])
        P.op("scalar", lambda s, sdp=sdp, mvp=mvp: s.activation(sdp[:], mvp[:, 1:2], AF.Sqrt,
                                                                bias=c_eps[:, 0:1]),
             reads=[rs, r_const], writes=[rs])
        P.op("vector", lambda v, rsp=rsp, sdp=sdp: v.reciprocal(rsp[:], sdp[:]),
             reads=[rs], writes=[rs])
        if bank_tile is not None:
            P.op("vector", lambda v, t=bank_tile, mvp=mvp: v.tensor_copy(ln1m[:, t:t + 1],
                                                                         mvp[:, 0:1]),
                 reads=[rs], writes=[r_ln1])
            P.op("vector", lambda v, t=bank_tile, rsp=rsp: v.tensor_copy(ln1r[:, t:t + 1],
                                                                         rsp[:]),
                 reads=[rs], writes=[r_ln1])
        return mvp, rsp, rs

    def transpose_tile(src_ap, src_res, dst_sb, dst_res, evac_eng="scalar", width=D):
        nhalf = width // 1024
        for half in range(nhalf):
            def f(t, src_ap=src_ap, half=half):
                ins = None
                for j in range(8):
                    kc = half * 8 + j
                    ins = t.transpose(psA[:, half * 1024 + j * 128:half * 1024 + (j + 1) * 128],
                                      src_ap[:, kc * 128:(kc + 1) * 128], ident[:])
                return ins
            P.op("tensor", f, reads=[src_res, r_ident], writes=[r_A[half]])
            if evac_eng == "scalar":
                P.op("scalar", lambda s, dst_sb=dst_sb, half=half: s.copy(
                    dst_sb[:, half * 1024:(half + 1) * 1024],
                    psA[:, half * 1024:(half + 1) * 1024]),
                    reads=[r_A[half]], writes=[dst_res, r_A[half]])
            else:
                P.op("vector", lambda v, dst_sb=dst_sb, half=half: v.tensor_copy(
                    dst_sb[:, half * 1024:(half + 1) * 1024],
                    psA[:, half * 1024:(half + 1) * 1024]),
                    reads=[r_A[half]], writes=[dst_res, r_A[half]])

    def proj_mms(xT_ap, xT_res, w_sb, w_res, psum, psum_res, ns=(0, 4)):
        for kc in range(KC):
            def f(t, xT_ap=xT_ap, w_sb=w_sb, psum=psum, kc=kc, ns=ns):
                ins = None
                for n in range(ns[0], ns[1]):
                    ins = t.matmul(psum[:, n * 512:(n + 1) * 512], xT_ap[:, kc, :],
                                   w_sb[:, kc, n * 512:(n + 1) * 512],
                                   start=(kc == 0), stop=(kc == KC - 1))
                return ins
            P.op("tensor", f, reads=[xT_res, w_res[kc]], writes=psum_res)

    def lnab_bcast():
        P.op("sync", lambda s: s.dma_start(out=fpool[0][0:1, :], in_=lng_d[:, :]), dma=True,
             writes=[r_fp[0]])
        def bc_mm(tt, srcrow):
            ins = None
            for n in range(4):
                ins = tt.matmul(psA[:, n * 512:(n + 1) * 512], ones_row[:],
                                srcrow[:, n * 512:(n + 1) * 512], start=True, stop=True)
            return ins
        P.op("tensor", lambda t: bc_mm(t, fpool[0][0:1, :]),
             reads=[r_fp[0], r_const], writes=[r_A[0], r_A[1]])
        P.op("vector", lambda v: v.tensor_copy(lng_bc, psA[:]),
             reads=[r_A[0], r_A[1]], writes=[r_stg[0], r_A[0], r_A[1]])
        P.op("sync", lambda s: s.dma_start(out=fpool[0][0:1, :], in_=lnb_d[:, :]), dma=True,
             reads=[r_fp[0]], writes=[r_fp[0]])
        P.op("tensor", lambda t: bc_mm(t, fpool[0][0:1, :]),
             reads=[r_fp[0], r_const], writes=[r_A[0], r_A[1]])
        P.op("vector", lambda v: v.tensor_copy(lnb_bc, psA[:]),
             reads=[r_A[0], r_A[1]], writes=[r_stg[0], r_A[0], r_A[1], r_fp[0]])

    def carry_prologue():
        for t in range(NT):
            fx = t % 2
            P.op("sync", lambda s, t=t, fx=fx: s.dma_start(
                out=fpool[fx][:], in_=xprev_d[t * 128:(t + 1) * 128, :]),
                dma=True, writes=[r_fp[fx]])
            mvp, rsp, rs = ln_stats(fpool[fx][:], r_fp[fx], par=fx)
            P.op("vector", lambda v, fx=fx, mvp=mvp, rsp=rsp: v.tensor_scalar(
                fpool[2][:], fpool[fx][:], mvp[:, 0:1], rsp[:], ALU.subtract, ALU.mult),
                reads=[r_fp[fx], rs], writes=[r_fp[2], r_fp[fx]])
            P.op("vector", lambda v: v.tensor_copy(bpool[0][:], fpool[2][:]),
                 reads=[r_fp[2]], writes=[r_bp[0], r_fp[2]])
            P.op("gpsimd", lambda g, t=t: g.iota(wlamT[:], pattern=[[-1, 128]],
                                                 base=2047 - 128 * t, channel_multiplier=0,
                                                 allow_small_or_imprecise_dtypes=True),
                 writes=[r_prol])
            P.op("scalar", lambda s: s.activation(wlamT[:], wlamT[:], AF.Exp,
                                                  scale=loglam_col[:, 0:1]),
                 reads=[r_prol, r_decay], writes=[r_prol])
            P.op("tensor", lambda tt: tt.transpose(psB[:, 0:16], wlamT[:], ident[0:16, 0:16]),
                 reads=[r_prol, r_ident], writes=[r_B[0]])
            P.op("scalar", lambda s: s.copy(bpool[1][:, 0:16], psB[:, 0:16]),
                 reads=[r_B[0]], writes=[r_bp[1], r_B[0]])
            def f(tt, t=t):
                ins = None
                for n in range(4):
                    ins = tt.matmul(psA[0:NH, n * 512:(n + 1) * 512], bpool[1][:, 0:NH],
                                    bpool[0][:, n * 512:(n + 1) * 512],
                                    start=(t == 0), stop=(t == NT - 1))
                return ins
            P.op("tensor", f, reads=[r_bp[0], r_bp[1]], writes=[r_A[0], r_A[1]])
        P.op("scalar", lambda s: s.copy(fpool[2][0:NH, :], psA[0:NH, :]),
             reads=[r_A[0], r_A[1]], writes=[r_fp[2], r_A[0], r_A[1]])

        def trx(tt):
            ins = None
            for kc in range(KC):
                ins = tt.transpose(psB[:, kc * 16:(kc + 1) * 16],
                                   fpool[2][0:NH, kc * 128:(kc + 1) * 128], ident[0:16, 0:16])
            return ins
        P.op("tensor", trx, reads=[r_fp[2], r_ident], writes=[r_B[0]])
        P.op("scalar", lambda s: s.copy(xbarT[:].rearrange("p a b -> p (a b)"),
                                        psB[:, 0:KC * NH]),
             reads=[r_B[0]], writes=[r_prol, r_B[0], r_fp[2]])
        for h in range(NH):
            for kc in range(KC):
                sub = bpool[2][:, (kc % KC) * 128:(kc % KC + 1) * 128]
                src = xbarT[:, kc, h:h + 1]
                bc_ap = bcast_last(src, 128)
                P.op("vector", lambda v, sub=sub, bc_ap=bc_ap: v.tensor_copy(
                    sub.rearrange("p (a b) -> p a b", b=128), bc_ap),
                    reads=[r_prol], writes=[r_bp[2]])
                def f(tt, h=h, kc=kc, sub=sub):
                    return tt.matmul(psB[:, h * 128:(h + 1) * 128], sub,
                                     wslotB[:, kc, h * 128:(h + 1) * 128],
                                     start=(kc == 0), stop=(kc == KC - 1))
                P.op("tensor", f, reads=[r_bp[2], r_wB[kc]],
                     writes=[r_B[h // 8]])
        P.op("scalar", lambda s: s.copy(carry[:], psB[:]),
             reads=[r_B[0], r_B[1]], writes=[r_carry, r_B[0], r_B[1]])

    def attn_qv_scan():
        def prep(t):
            fx = t % 2
            P.op("sync", lambda s, t=t, fx=fx: s.dma_start(
                out=fpool[fx][:], in_=x_d[t * 128:(t + 1) * 128, :]), dma=True,
                writes=[r_fp[fx]])
            mvp, rsp, rs = ln_stats(fpool[fx][:], r_fp[fx], bank_tile=t, par=fx)
            P.op("vector", lambda v, fx=fx, mvp=mvp, rsp=rsp: v.tensor_scalar(
                fpool[2][:], fpool[fx][:], mvp[:, 0:1], rsp[:], ALU.subtract, ALU.mult),
                reads=[r_fp[fx], rs], writes=[r_fp[2]])

        prep(0)
        for t in range(NT):
            fx = t % 2
            bx = t % 2
            transpose_tile(fpool[2][:], r_fp[2], bpool[bx], r_bp[bx])
            xT = bpool[bx][:].rearrange("p (a b) -> p a b", b=128)
            proj_mms(xT, r_bp[bx], wslotA, r_wA, psB, [r_B[0], r_B[1]])
            proj_mms(xT, r_bp[bx], wslotB, r_wB, psA, [r_A[0], r_A[1]])
            if t + 1 < NT:
                prep(t + 1)
            P.op("scalar", lambda s: s.copy(vst[:], psA[:]),
                 reads=[r_A[0], r_A[1]], writes=[r_vst, r_A[0], r_A[1]])

            def scan_f(tt):
                ins = None
                for h in range(NH):
                    ins = tt.matmul(psA[:, h * 128:(h + 1) * 128], t_mat[:, h, :],
                                    vst[:, h * 128:(h + 1) * 128], start=True, stop=False)
                    ins = tt.matmul(psA[:, h * 128:(h + 1) * 128], l3_mat[:, h, :],
                                    carry[:, h * 128:(h + 1) * 128], start=False, stop=True)
                return ins
            P.op("tensor", scan_f, reads=[r_vst, r_carry, r_decay], writes=[r_A[0], r_A[1]])
            P.op("scalar", lambda s: s.copy(vst[:], psA[:]),
                 reads=[r_A[0], r_A[1]], writes=[r_vst])
            P.op("vector", lambda v: v.tensor_tensor(stg[1][:], psB[:], vst[:],
                                                     ALU.mult),
                 reads=[r_B[0], r_B[1], r_vst], writes=[r_stg[1], r_B[0], r_B[1]])
            if t + 1 < NT:
                def ext_f(tt):
                    ins = None
                    for n in range(4):
                        ins = tt.matmul(psA[:, n * 512:(n + 1) * 512], e127[:],
                                        vst[:, n * 512:(n + 1) * 512], start=True, stop=True)
                    return ins
                P.op("tensor", ext_f, reads=[r_vst, r_decay], writes=[r_A[0], r_A[1]])
                P.op("scalar", lambda s: s.copy(carry[:], psA[:]),
                     reads=[r_A[0], r_A[1]], writes=[r_carry, r_A[0], r_A[1]])
            mvp, rsp, rs = ln_stats(stg[1][:], r_stg[1], par=fx)
            P.op("vector", lambda v, mvp=mvp, rsp=rsp: v.tensor_scalar(
                stg[1][:], stg[1][:], mvp[:, 0:1], rsp[:],
                ALU.subtract, ALU.mult),
                 reads=[rs], writes=[r_stg[1]])
            P.op("vector", lambda v: v.tensor_tensor(stg[1][:], stg[1][:],
                                                     lng_bc, ALU.mult),
                 reads=[r_stg[0]], writes=[r_stg[1]])
            P.op("vector", lambda v: v.tensor_tensor(bpool[2][:], stg[1][:],
                                                     lnb_bc, ALU.add),
                 reads=[r_stg[0], r_stg[1]], writes=[r_bp[2], r_stg[1]])
            P.op("gpsimd", lambda g, t=t: g.dma_start(out=s_yaff[t], in_=bpool[2][:]),
                 dma=True, reads=[r_bp[2]], writes=[r_syaff[t]])

    def attn_gate():
        load_square_weight(wg_d, wslotA, r_wA)

        def drain_ygT(t):
            transpose_tile(fpool[t % 2][:], r_fp[t % 2], bpool[3], r_bp[3],
                           evac_eng="vector")
            P.op("gpsimd", lambda g, t=t: g.dma_start(out=s_ygT[t], in_=bpool[3][:]),
                 dma=True, reads=[r_bp[3]], writes=[r_sygT[t]])

        def prep(t):
            fx = t % 2
            P.op("sync", lambda s, t=t, fx=fx: s.dma_start(
                out=fpool[fx][:], in_=x_d[t * 128:(t + 1) * 128, :]), dma=True,
                writes=[r_fp[fx]])
            P.op("vector", lambda v, fx=fx, t=t: v.tensor_scalar(
                fpool[2][:], fpool[fx][:], ln1m[:, t:t + 1], ln1r[:, t:t + 1],
                ALU.subtract, ALU.mult),
                reads=[r_fp[fx], r_ln1], writes=[r_fp[2]])

        prep(0)
        pend = None
        for t in range(NT):
            fx = t % 2
            bx = t % 2
            transpose_tile(fpool[2][:], r_fp[2], bpool[bx], r_bp[bx])
            xT = bpool[bx][:].rearrange("p (a b) -> p a b", b=128)
            proj_mms(xT, r_bp[bx], wslotA, r_wA, psB, [r_B[0]], ns=(0, 2))
            P.op("scalar", lambda s, fx=fx: s.activation(fpool[fx][:, 0:1024],
                                                         psB[:, 0:1024], AF.Silu),
                 reads=[r_B[0]], writes=[r_fp[fx], r_B[0]])
            proj_mms(xT, r_bp[bx], wslotA, r_wA, psB, [r_B[1]], ns=(2, 4))
            P.op("scalar", lambda s, fx=fx: s.activation(fpool[fx][:, 1024:2048],
                                                         psB[:, 1024:2048], AF.Silu),
                 reads=[r_B[1]], writes=[r_fp[fx], r_B[1]])
            if pend is not None:
                drain_ygT(pend)
            if t + 1 < NT:
                prep(t + 1)
            P.op("gpsimd", lambda g, t=t: g.dma_start(out=bpool[2][:], in_=s_yaff[t]),
                 dma=True, reads=[r_syaff[t]], writes=[r_bp[2]])
            P.op("vector", lambda v, fx=fx: v.tensor_tensor(fpool[fx][:], bpool[2][:],
                                                            fpool[fx][:], ALU.mult),
                 reads=[r_bp[2]], writes=[r_fp[fx], r_bp[2]])
            pend = t
        drain_ygT(pend)

    def attn_out():
        load_square_weight(wo_d, wslotB, r_wB)

        def drain_xn3T(t):
            bx = t % 2
            pso, rpso = (psA, r_A) if t % 2 == 0 else (psB, r_B)
            for half in range(2):
                def ftr(tt, half=half, pso=pso):
                    ins = None
                    for j in range(8):
                        kc = half * 8 + j
                        ins = tt.transpose(
                            pso[:, half * 1024 + j * 128:half * 1024 + (j + 1) * 128],
                            fpool[2][:, kc * 128:(kc + 1) * 128], ident[:])
                    return ins
                P.op("tensor", ftr, reads=[r_fp[2], r_ident], writes=[rpso[half]])
                P.op("scalar", lambda s, half=half, pso=pso: s.copy(
                    bpool[2][:, half * 1024:(half + 1) * 1024],
                    pso[:, half * 1024:(half + 1) * 1024]),
                    reads=[rpso[half]], writes=[r_bp[2], rpso[half]])
            P.op("gpsimd", lambda g, t=t: g.dma_start(out=s_xn3T[t], in_=bpool[2][:]),
                 dma=True, reads=[r_bp[2]], writes=[r_sxn3T[t]])

        pend = None
        P.op("sync", lambda s: s.dma_start(out=bpool[0][:], in_=s_ygT[0]),
             dma=True, reads=[r_sygT[0]], writes=[r_bp[0]])
        for t in range(NT):
            bx = t % 2
            fx = t % 2
            ygT = bpool[bx][:].rearrange("p (a b) -> p a b", b=128)
            pso, rpso = (psA, r_A) if t % 2 == 0 else (psB, r_B)
            proj_mms(ygT, r_bp[bx], wslotB, r_wB, pso, [rpso[0], rpso[1]])
            if t + 1 < NT:
                P.op("sync", lambda s, t=t: s.dma_start(out=bpool[(t + 1) % 2][:],
                                                        in_=s_ygT[t + 1]),
                     dma=True, reads=[r_sygT[t + 1]], writes=[r_bp[(t + 1) % 2]])
            if pend is not None:
                drain_xn3T(pend)
            mvp, rsp, rs = ln_stats(pso[:], rpso[0], par=t % 2)
            P.op("vector", lambda v, pso=pso, mvp=mvp, rsp=rsp: v.tensor_scalar(
                fpool[2][:], pso[:], mvp[:, 0:1], rsp[:],
                ALU.subtract, ALU.mult),
                 reads=[rs, rpso[0], rpso[1]],
                 writes=[r_fp[2], rpso[0], rpso[1]])
            P.op("sync", lambda s, t=t, fx=fx: s.dma_start(
                out=fpool[fx][:], in_=x_d[t * 128:(t + 1) * 128, :]), dma=True,
                writes=[r_fp[fx]])
            P.op("vector", lambda v, fx=fx: v.tensor_tensor(fpool[fx][:], fpool[2][:],
                                                            fpool[fx][:], ALU.add),
                 reads=[r_fp[2]], writes=[r_fp[fx], r_fp[2]])
            P.op("gpsimd", lambda g, t=t, fx=fx: g.dma_start(out=s_x2[t], in_=fpool[fx][:]),
                 dma=True, reads=[r_fp[fx]], writes=[r_sx2[t]])
            mvp, rsp, rs = ln_stats(fpool[fx][:], r_fp[fx], par=t % 2)
            P.op("vector", lambda v, fx=fx, mvp=mvp, rsp=rsp: v.tensor_scalar(
                fpool[2][:], fpool[fx][:], mvp[:, 0:1], rsp[:], ALU.subtract, ALU.mult),
                reads=[r_fp[fx], rs], writes=[r_fp[2], r_fp[fx]])
            pend = t
        drain_xn3T(pend)

    wstg4 = [stg[0][:][:, 0:1024], stg[0][:][:, 1024:2048],
             stg[1][:][:, 0:1024], stg[1][:][:, 1024:2048]]
    r_wstg4 = resarr(4)
    wrr = [0]

    def load_half(dst_ap, dst_res, src_ap, couple):
        k = wrr[0] % 4
        wrr[0] += 1
        extra = [r_stg[k // 2]] if couple else []
        P.op("sync", lambda s, src_ap=src_ap, k=k: s.dma_start(
            out=wstg4[k], in_=src_ap), dma=True, writes=[r_wstg4[k]] + extra)
        if wrr[0] % 2 == 0:
            P.op("vector", lambda v, dst_ap=dst_ap, k=k: v.tensor_copy(
                dst_ap, wstg4[k]),
                reads=[r_wstg4[k]], writes=[dst_res, r_wstg4[k]])
        else:
            P.op("scalar", lambda s, dst_ap=dst_ap, k=k: s.copy(
                dst_ap, wstg4[k]),
                reads=[r_wstg4[k]], writes=[dst_res, r_wstg4[k]])

    def load_cast(dst_ap, dst_res, src_ap, width):
        for hh in range(width // 1024):
            load_half(dst_ap[:, hh * 1024:(hh + 1) * 1024], dst_res,
                      src_ap[:, hh * 1024:(hh + 1) * 1024], False)

    def load_q(ic, q):
        for kc in range(q * 4, (q + 1) * 4):
            load_cast(fgc[:, kc, :], r_fgc[q],
                      fg_d[kc * 128:(kc + 1) * 128, ic * ICW:(ic + 1) * ICW], ICW)
            load_cast(fuc[:, kc, :], r_fuc[q],
                      fu_d[kc * 128:(kc + 1) * 128, ic * ICW:(ic + 1) * ICW], ICW)

    def load_fdc(ic, cs, ce):
        for c in range(cs, ce):
            load_cast(fdc[:, c, :], r_fdc[c // 2],
                      fd_d[(ic * ICW + c * 128):(ic * ICW + (c + 1) * 128), :], D)

    def ffn():
        hbuf = [p0buf[0].bitcast(F32), p0buf[1].bitcast(F32),
                fpool[2][:][:, 0:ICW]]
        hflats = [p0buf[0], p0buf[1], fpool[2][:].bitcast(BF16)[:, 0:2 * ICW]]
        r_h = [r_p0b[0], r_p0b[1], r_fp[2]]
        drr = [0]

        def emit_ln4(t):
            P.op("sync", lambda s, t=t: s.dma_start(out=fpool[2][:], in_=s_acc[t]),
                 dma=True, reads=[r_sacc[t]], writes=[r_fp[2]])
            mvp, rsp, rs = ln_stats(fpool[2][:], r_fp[2], par=t % 2)
            fx = t % 2
            P.op("sync", lambda s, t=t, fx=fx: s.dma_start(out=fpool[fx][:], in_=s_x2[t]),
                 dma=True, reads=[r_sx2[t]], writes=[r_fp[fx]])
            P.op("vector", lambda v, fx=fx, mvp=mvp, rsp=rsp: v.tensor_scalar(
                fpool[2][:], fpool[2][:], mvp[:, 0:1], rsp[:],
                ALU.subtract, ALU.mult),
                 reads=[rs], writes=[r_fp[2]])
            P.op("vector", lambda v, fx=fx: v.tensor_tensor(fpool[2][:], fpool[2][:],
                                                            fpool[fx][:], ALU.add),
                 reads=[r_fp[fx]], writes=[r_fp[2], r_fp[fx]])
            yres = Res()
            tok = P.op("sync", lambda s, t=t: s.dma_start(
                out=y_d[t * 128:(t + 1) * 128, :], in_=fpool[2][:]),
                dma=True, reads=[r_fp[2]], writes=[yres])
            out_toks.append(tok)

        def drain(pend):
            pt, pic = pend
            par = ((pic * NT + pt) % 3) if pic < NIC - 1 else (pt % 2)
            hsp = hbuf[par]
            hflat = hflats[par]
            rh = r_h[par]

            def trf(tt, hsp=hsp):
                ins = None
                for j in range(8):
                    ins = tt.transpose(psA[:, j * 128:(j + 1) * 128],
                                       hsp[:, j * 128:(j + 1) * 128], ident[:])
                return ins
            P.op("tensor", trf, reads=[rh, r_ident], writes=[r_A[0]])
            hT_flat = hflat
            rc = [Res(), Res()]
            for hh in range(2):
                P.op("scalar", lambda s, hT_flat=hT_flat, hh=hh: s.copy(
                    hT_flat[:, hh * 512:(hh + 1) * 512],
                    psA[:, hh * 512:(hh + 1) * 512]),
                     reads=[r_A[0]], writes=[rh, rc[hh], r_A[0]])
            hT = hT_flat[:, 0:ICW].rearrange("p (a b) -> p a b", b=128)
            for q in range(4):
                def f(tt, q=q, hT=hT):
                    ins = None
                    for kc in range(q * 2, (q + 1) * 2):
                        for n in range(4):
                            ins = tt.matmul(psB[:, n * 512:(n + 1) * 512], hT[:, kc, :],
                                            fdc[:, kc, n * 512:(n + 1) * 512],
                                            start=(kc == 0), stop=(kc == ICW // 128 - 1))
                    return ins
                P.op("tensor", f, reads=[rh, rc[q // 2], r_fdc[q]],
                     writes=[r_B[0], r_B[1]])
            dk = drr[0] % 4
            drr[0] += 1
            ev = stgF[dk]
            P.op("vector", lambda v, ev=ev: v.tensor_copy(ev[:], psB[:]),
                 reads=[r_B[0], r_B[1]], writes=[r_stgF[dk], r_B[0], r_B[1]])
            if pic == 0:
                P.op("gpsimd", lambda g, pt=pt, ev=ev: g.dma_start(
                    out=s_acc[pt], in_=ev[:]),
                    dma=True, reads=[r_stgF[dk]], writes=[r_sacc[pt]])
            else:
                P.op("gpsimd", lambda g, pt=pt, ev=ev: g.dma_start(
                    out=s_acc[pt], in_=ev[:], accum_op=ALU.add),
                    dma=True, reads=[r_stgF[dk]], writes=[r_sacc[pt]])
            if pic == NIC - 1:
                emit_ln4(pt)

        pending = []
        P.op("sync", lambda s: s.dma_start(out=bpool[0][:], in_=s_xn3T[0]),
             dma=True, reads=[r_sxn3T[0]], writes=[r_bp[0]])
        for ic in range(NIC):
            while pending:
                drain(pending.pop(0))
            for t in range(NT):
                bx = t % 2
                fx = t % 2
                xT = bpool[bx][:].rearrange("p (a b) -> p a b", b=128)
                for q in range(4):
                    def f(tt, xT=xT, q=q):
                        ins = None
                        for kc in range(q * 4, (q + 1) * 4):
                            for n in range(2):
                                ins = tt.matmul(psA[:, n * 512:(n + 1) * 512], xT[:, kc, :],
                                                fgc[:, kc, n * 512:(n + 1) * 512],
                                                start=(kc == 0), stop=(kc == KC - 1))
                        return ins
                    P.op("tensor", f, reads=[r_bp[bx], r_fgc[q]], writes=[r_A[0]])
                P.op("scalar", lambda s, fx=fx: s.activation(fpool[fx][:, 0:ICW],
                                                             psA[:, 0:ICW], AF.Silu),
                     reads=[r_A[0]], writes=[r_fp[fx], r_A[0]])
                for q in range(4):
                    def f(tt, xT=xT, q=q):
                        ins = None
                        for kc in range(q * 4, (q + 1) * 4):
                            for n in range(2):
                                ins = tt.matmul(psA[:, 1024 + n * 512:1024 + (n + 1) * 512],
                                                xT[:, kc, :], fuc[:, kc, n * 512:(n + 1) * 512],
                                                start=(kc == 0), stop=(kc == KC - 1))
                        return ins
                    P.op("tensor", f, reads=[r_bp[bx], r_fuc[q]], writes=[r_A[1]])
                    if t == NT - 1 and ic + 1 < NIC:
                        load_q(ic + 1, q)
                if t < 2:
                    load_fdc(ic, t * 4, (t + 1) * 4)
                if ic * NT + t + 1 < NIC * NT:
                    nt_ = (t + 1) % NT
                    P.op("sync", lambda s, nt_=nt_: s.dma_start(
                        out=bpool[nt_ % 2][:], in_=s_xn3T[nt_]),
                         dma=True, reads=[r_sxn3T[nt_]], writes=[r_bp[nt_ % 2]])
                hpar = ((ic * NT + t) % 3) if ic < NIC - 1 else (t % 2)
                hsl = hbuf[hpar]
                rht = r_h[hpar]
                P.op("vector", lambda v, fx=fx, hsl=hsl: v.tensor_tensor(
                    hsl[:, 0:ICW], psA[:, ICW:2 * ICW],
                    fpool[fx][:, 0:ICW], ALU.mult),
                     reads=[r_A[1], r_fp[fx]], writes=[rht, r_A[1], r_fp[fx]])
                maxp = 2 if ic < NIC - 1 else 1
                if len(pending) >= maxp:
                    drain(pending.pop(0))
                habs = bpool[2]
                P.op("scalar", lambda s, hsl=hsl: s.activation(habs[:, 0:ICW],
                                                               hsl[:, 0:ICW], AF.Abs),
                     reads=[rht], writes=[r_bp[2]])
                a4 = habs[:, 0:ICW].rearrange("p (g k) -> p g k", k=4)
                G = ICW // 4
                m2t = bpool[3]
                mn1 = m2t[:, 0:G].rearrange("p (g o) -> p g o", o=1)
                mx1 = m2t[:, G:2 * G].rearrange("p (g o) -> p g o", o=1)
                mn2 = m2t[:, 2 * G:3 * G].rearrange("p (g o) -> p g o", o=1)
                mx2 = m2t[:, 3 * G:4 * G].rearrange("p (g o) -> p g o", o=1)
                P.op("vector", lambda v: v.tensor_tensor(mn1, a4[:, :, 0:1], a4[:, :, 1:2],
                                                         ALU.min),
                     reads=[r_bp[2]], writes=[r_bp[3]])
                P.op("vector", lambda v: v.tensor_tensor(mx1, a4[:, :, 0:1], a4[:, :, 1:2],
                                                         ALU.max),
                     reads=[r_bp[2]], writes=[r_bp[3]])
                P.op("vector", lambda v: v.tensor_tensor(mn2, a4[:, :, 2:3], a4[:, :, 3:4],
                                                         ALU.min),
                     reads=[r_bp[2]], writes=[r_bp[3]])
                P.op("vector", lambda v: v.tensor_tensor(mx2, a4[:, :, 2:3], a4[:, :, 3:4],
                                                         ALU.max),
                     reads=[r_bp[2]], writes=[r_bp[3]])
                P.op("vector", lambda v: v.tensor_tensor(mn1, mn1, mn2, ALU.max),
                     reads=[r_bp[3]], writes=[r_bp[3]])
                P.op("vector", lambda v: v.tensor_tensor(mx1, mx1, mx2, ALU.min),
                     reads=[r_bp[3]], writes=[r_bp[3]])
                P.op("vector", lambda v: v.tensor_tensor(mn1, mn1, mx1, ALU.min),
                     reads=[r_bp[3]], writes=[r_bp[3]])
                m2b = bcast_last(m2t[:, 0:G].rearrange("p (g o) -> p g o", o=1), 4)
                cmpm = bpool[3][:, ICW:2 * ICW]
                P.op("vector", lambda v, m2b=m2b, cmpm=cmpm: v.tensor_tensor(
                    cmpm.rearrange("p (g k) -> p g k", k=4), a4, m2b, ALU.is_gt),
                    reads=[r_bp[2]], writes=[r_bp[3]])
                P.op("vector", lambda v, cmpm=cmpm, hsl=hsl: v.tensor_tensor(
                    hsl[:, 0:ICW], hsl[:, 0:ICW], cmpm, ALU.mult),
                     reads=[r_bp[3]], writes=[rht])
                pending.append((t, ic))
        while pending:
            drain(pending.pop(0))

    setup()
    load_square_weight(wq_d, wslotA, r_wA)
    load_square_weight(wv_d, wslotB, r_wB, cast_eng="scalar")
    lnab_bcast()
    carry_prologue()
    attn_qv_scan()
    attn_gate()
    handoff([r_fgc[0], r_fgc[1], r_fgc[2], r_fgc[3], r_fuc[0], r_fuc[1], r_fuc[2], r_fuc[3]],
            r_wA)
    for _q in range(4):
        load_q(0, _q)
    attn_out()
    handoff([r_fdc[0], r_fdc[1], r_fdc[2], r_fdc[3]] + r_stgF, r_wB)
    handoff([r_p0b[0], r_p0b[1]], [r_decay])
    ffn()
    P.barrier_wait("sync", out_toks)

    P.realize(nc, es)
    es.close()
    return nc


_NC = None


def get_nc():
    global _NC
    if _NC is None:
        _NC = build()
    return _NC


def kernel(L0_attn_x, L0_attn_Wq, L0_attn_Wv, L0_attn_Wg, L0_attn_Wo, L0_attn_beta,
           L0_attn_ln_g, L0_attn_ln_b, L0_ffn_w_gate, L0_ffn_w_up, L0_ffn_w_down):
    x = np.ascontiguousarray(np.asarray(L0_attn_x, dtype=np.float32))
    B, S, _ = x.shape
    half = S // 2
    beta = np.ascontiguousarray(L0_attn_beta, dtype=np.float32)
    weights = dict(
        Wq=np.ascontiguousarray(L0_attn_Wq, dtype=np.float32),
        Wv=np.ascontiguousarray(L0_attn_Wv, dtype=np.float32),
        Wg=np.ascontiguousarray(L0_attn_Wg, dtype=np.float32),
        Wo=np.ascontiguousarray(L0_attn_Wo, dtype=np.float32),
        w_gate=np.ascontiguousarray(L0_ffn_w_gate, dtype=np.float32),
        w_up=np.ascontiguousarray(L0_ffn_w_up, dtype=np.float32),
        w_down=np.ascontiguousarray(L0_ffn_w_down, dtype=np.float32),
        beta_r=beta.reshape(1, NH),
        beta_c=np.ascontiguousarray(beta.reshape(NH, 1)),
        ln_g=np.ascontiguousarray(np.asarray(L0_attn_ln_g, dtype=np.float32).reshape(1, D)),
        ln_b=np.ascontiguousarray(np.asarray(L0_attn_ln_b, dtype=np.float32).reshape(1, D)),
    )
    zeros = np.zeros((half, D), dtype=np.float32)
    in_maps = []
    for b in range(B):
        for hf in range(2):
            m = dict(weights)
            m["x"] = np.ascontiguousarray(x[b, hf * half:(hf + 1) * half])
            m["xprev"] = np.ascontiguousarray(x[b, 0:half]) if hf == 1 else zeros
            in_maps.append(m)
    nc = get_nc()
    res = run_bass_kernel_spmd(nc, in_maps, core_ids=list(range(8)))
    out = np.empty((B, S, D), dtype=np.float32)
    for i, r in enumerate(res.results):
        b, hf = divmod(i, 2)
        out[b, hf * half:(hf + 1) * half] = r["y"]
    return out
